# version 07
# baseline (speedup 1.0000x reference)
import numpy as np
import ml_dtypes
import re
import contextlib
import concourse.bass as bass
import concourse.mybir as mybir
from concourse.bass import AP
from concourse.bass_utils import run_bass_kernel_spmd

F32 = mybir.dt.float32
BF16 = mybir.dt.bfloat16
U32 = mybir.dt.uint32
AF = mybir.ActivationFunctionType
ALU = mybir.AluOpType
AX = mybir.AxisListType

SEQ = 4096
D = 1024
NT = 32
HCH = 8
NEG = -1.0e9

DBG = set()
BUILD_STAGE = 99


class Kc:

    def __init__(self, nc, stack, block):
        self.nc = nc
        self.stack = stack
        self.block = block
        self._sems = {}
        self.dbg_tensors = {}

    def sb(self, name, shape, dt=F32):
        return self.stack.enter_context(self.nc.sbuf_tensor(name, shape, dt))

    def shared(self, name, shape, dt=F32):
        if not hasattr(self, "_shared"):
            self._shared = {}
        if name not in self._shared:
            self._shared[name] = (self.sb(name, shape, dt), tuple(shape), dt)
        t, sh, d = self._shared[name]
        assert sh == tuple(shape) and d == dt, f"shared {name} shape/dtype mismatch"
        return t

    def ps(self, name, shape, dt=F32):
        return self.stack.enter_context(self.nc.psum_tensor(name, shape, dt))

    def sem(self, name):
        if not hasattr(self, "_roles"):
            self._roles = {}
        if name not in self._sems:
            role = re.sub(r"^s\d+", "", name)
            if role not in self._roles:
                self._roles[role] = [self.stack.enter_context(self.nc.semaphore(role)), 0]
            ph = self._roles[role]
            self._sems[name] = [ph, ph[1], 0]
        return self._sems[name]

    def inc(self, instr, name, k=1):
        s = self.sem(name)
        instr.then_inc(s[0][0], k)
        s[0][1] += k
        s[2] += k
        return instr

    def val(self, name):
        return self.sem(name)[2]

    def wait(self, eng, name, v=None):
        s = self.sem(name)
        eng.wait_ge(s[0][0], s[1] + (s[2] if v is None else v))

    def wait_role(self, eng, role):
        if hasattr(self, "_roles") and role in self._roles:
            ph = self._roles[role]
            eng.wait_ge(ph[0], ph[1])

    def dbg(self, name, shape, dt=F32):
        if name not in DBG:
            return None
        t = self.nc.dram_tensor("dbg_" + name, shape, dt, kind="ExternalOutput").ap()
        self.dbg_tensors[name] = t
        return t

    def dbg_store(self, sync, name, src_ap):
        if name in self.dbg_tensors:
            self.inc(sync.dma_start(out=self.dbg_tensors[name], in_=src_ap), "dbgst", 16)


def emit_ln(c, si, x_src, xhatT, pT2, after=None):
    nc = c.nc
    p = f"s{si}ln"
    xta = c.shared("ln_xta", [128, 2 * D])
    xtb = c.shared("ln_xtb", [128, 2 * D])
    stats = c.shared("ln_xtb", [128, 2 * D])[:, 0:NT * 12]
    mv = c.shared("ln_mv", [128, NT * 2])
    varv = c.shared("ln_varv", [128, NT])
    rstd = c.shared("ln_rstd", [128, NT])
    nbm = c.shared("ln_nbias", [128, NT])
    xhat = c.shared("ln_xhat", [128, 2 * D], BF16)
    ident = c.ident

    ld0 = [c.val(p + "ld0"), c.val(p + "ld1")]

    @c.block.sync
    def _(sync):
        if after is not None:
            c.wait(sync, after[0], after[1])
        for t in range(NT):
            if t >= 2:
                c.wait(sync, p + "st", t - 1)
            c.inc(sync.dma_start(out=xta[:, (t % 2) * D:(t % 2 + 1) * D],
                                 in_=x_src[128 * t:128 * (t + 1), :]), p + "ld" + str(t % 2), 16)

    @c.block.vector
    def _(vector):
        for t in range(NT):
            c.wait(vector, p + "ld" + str(t % 2), ld0[t % 2] + 16 * (t // 2 + 1))
            xt = xta[:, (t % 2) * D:(t % 2 + 1) * D]
            nc.vector.bn_stats(stats[:, 12 * t:12 * t + 6], xt[:, 0:512])
            c.inc(nc.vector.bn_stats(stats[:, 12 * t + 6:12 * t + 12], xt[:, 512:1024]),
                  p + "st", 1)
        nc.vector.drain()
        for t in range(NT):
            nc.vector.bn_aggr(mv[:, 2 * t:2 * t + 2], stats[:, 12 * t:12 * t + 12])
        nc.vector.drain()
        mvv = mv[:].rearrange("q (t two) -> q t two", two=2)
        nc.vector.tensor_scalar_add(varv[:], mvv[:, :, 1:2].squeeze(2), 1e-5)
        c.inc(nc.vector.drain(), p + "mv", 1)

    @c.block.scalar
    def _(scalar):
        c.wait(scalar, p + "mv")
        nc.scalar.activation(varv[:], varv[:], AF.Ln)
        nc.scalar.drain()
        nc.scalar.activation(rstd[:], varv[:], AF.Exp, scale=-0.5)
        c.inc(nc.scalar.drain(), p + "rstd", 1)

    @c.block.vector
    def _(vector):
        c.wait(vector, p + "rstd")
        mvv = mv[:].rearrange("q (t two) -> q t two", two=2)
        nc.vector.tensor_scalar_mul(nbm[:], mvv[:, :, 0:1].squeeze(2), -1.0)
        c.inc(nc.vector.drain(), p + "nb", 1)

    ld2_0 = [c.val(p + "ld2_0"), c.val(p + "ld2_1")]

    @c.block.sync
    def _(sync):
        c.wait(sync, p + "mv")
        for t in range(NT):
            if t >= 2:
                c.wait(sync, p + "xh", t - 1)
            c.inc(sync.dma_start(out=xtb[:, (t % 2) * D:(t % 2 + 1) * D],
                                 in_=x_src[128 * t:128 * (t + 1), :]), p + "ld2_" + str(t % 2), 16)

    @c.block.vector
    def _(vector):
        c.wait(vector, p + "nb")

        def evac(t):
            c.wait(vector, p + "tr", t + 1)
            dst = AP(xhatT, 128 * t, [[HCH * SEQ, 128], [SEQ, HCH], [1, 128]])
            src = pT2[t % 2][:].rearrange("q (h t) -> q h t", h=HCH)
            c.inc(nc.vector.tensor_copy(dst, src), p + "ev", 1)

        for t in range(NT):
            c.wait(vector, p + "ld2_" + str(t % 2), ld2_0[t % 2] + 16 * (t // 2 + 1))
            if t >= 2:
                c.wait(vector, p + "tr", t - 1)
            c.inc(nc.vector.tensor_scalar(xhat[:, (t % 2) * D:(t % 2 + 1) * D],
                                          xtb[:, (t % 2) * D:(t % 2 + 1) * D],
                                          nbm[:, t:t + 1], rstd[:, t:t + 1],
                                          ALU.add, ALU.mult),
                  p + "xh", 1)
            if t >= 1:
                evac(t - 1)
        evac(NT - 1)

    @c.block.tensor
    def _(tensor):
        for t in range(NT):
            c.wait(tensor, p + "xh", t + 1)
            if t >= 1:
                c.wait(tensor, p + "ev", t)
            pT = pT2[t % 2]
            for h in range(HCH):
                ins = nc.tensor.transpose(pT[:, 128 * h:128 * (h + 1)],
                                          xhat[:, (t % 2) * D + 128 * h:(t % 2) * D + 128 * (h + 1)],
                                          ident[:])
            c.inc(ins, p + "tr", 1)



    return dict(nbias=nbm, rstd=rstd)



def build_program(stage=99):
    nc = bass.Bass("TRN2", target_bir_lowering=False)
    x_in = nc.dram_tensor("x_in", [SEQ, D], F32, kind="ExternalInput").ap()
    w_inT = nc.dram_tensor("w_inT", [D, 3 * D], F32, kind="ExternalInput").ap()
    b_in = nc.dram_tensor("b_in", [128, 24], F32, kind="ExternalInput").ap()
    b_in_r = nc.dram_tensor("b_in_r", [24, 128], F32, kind="ExternalInput").ap()
    outwT = nc.dram_tensor("outwT", [D, D], F32, kind="ExternalInput").ap()
    out_b = nc.dram_tensor("out_b", [1, D], F32, kind="ExternalInput").ap()
    relpeT = nc.dram_tensor("relpeT", [64, 129], F32, kind="ExternalInput").ap()
    r_w = nc.dram_tensor("r_w", [D, 8], F32, kind="ExternalInput").ap()
    wq = {}
    for br in ("d", "m"):
        for m in ("q", "k", "v"):
            wq[f"w_{br}{m}"] = nc.dram_tensor(f"w_{br}{m}", [D, 512], F32, kind="ExternalInput").ap()
    wo_d = nc.dram_tensor("wo_d", [512, D], F32, kind="ExternalInput").ap()
    wo_m = nc.dram_tensor("wo_m", [512, D], F32, kind="ExternalInput").ap()
    cos_t = nc.dram_tensor("cos_t", [SEQ, 16], BF16, kind="ExternalInput").ap()
    sin_t = nc.dram_tensor("sin_t", [SEQ, 16], BF16, kind="ExternalInput").ap()
    ident_in = nc.dram_tensor("ident_in", [128, 128], BF16, kind="ExternalInput").ap()
    triu_in = nc.dram_tensor("triu_in", [128, 128], BF16, kind="ExternalInput").ap()
    ones1_in = nc.dram_tensor("ones1_in", [1, 128], F32, kind="ExternalInput").ap()
    bdiag_in = nc.dram_tensor("bdiag_in", [128, 128], BF16, kind="ExternalInput").ap()

    y = nc.dram_tensor("y", [SEQ, D], F32, kind="ExternalOutput").ap()
    xr1 = nc.dram_tensor("xr1", [SEQ, D], F32, kind="Internal").ap()
    xr2 = nc.dram_tensor("xr2", [SEQ, D], F32, kind="Internal").ap()
    shear_scr = nc.dram_tensor("shear_scr", [8, 128, 512], F32, kind="Internal")
    router_scr = nc.dram_tensor("router_scr", [8, SEQ], F32, kind="Internal")
    thr_scr = nc.dram_tensor("thr_scr", [128], F32, kind="Internal")

    with contextlib.ExitStack() as stack, nc.Block() as block:
        c = Kc(nc, stack, block)
        c.xhatT = c.sb("xhatT", [128, HCH * SEQ], BF16)
        c.big2 = c.sb("big2", [128, NT * 1024], BF16)
        c.ident = c.sb("ident", [128, 128], BF16)
        c.triu = c.sb("triu", [128, 128], BF16)
        c.ones1 = c.sb("ones1", [1, 128], F32)
        c.bdiag = c.sb("bdiag", [128, 128], BF16)
        c.pSa = c.ps("pSa", [128, 1024], F32)
        c.pSb = c.ps("pSb", [128, 1024], F32)
        c.pACC = c.ps("pACC", [128, 512], F32)
        c.pW = c.ps("pW", [128, 512], F32)
        c.pC = c.ps("pC", [128, 512], F32)
        c.pT = c.ps("pT", [128, 1024], BF16)
        c.pA = c.pSa[:, 0:512]
        c.pB = c.pSb[:, 0:512]
        c.pD = c.pSa[:, 512:1024]
        c.pTa = c.pT
        c.pTb = c.pT

        @c.block.sync
        def _(sync):
            c.inc(sync.dma_start(out=c.ident[:], in_=ident_in), "const", 16)
            c.inc(sync.dma_start(out=c.triu[:], in_=triu_in), "const", 16)
            c.inc(sync.dma_start(out=c.ones1[:], in_=ones1_in), "const", 16)
            c.inc(sync.dma_start(out=c.bdiag[:], in_=bdiag_in), "const", 16)

        @c.block.tensor
        def _(tensor):
            c.wait(tensor, "const")

        @c.block.vector
        def _(vector):
            c.wait(vector, "const")

        dbg_xhatT = c.dbg("xhatT", [128, HCH * SEQ], BF16)

        emit_ln(c, 0, x_in, c.xhatT, (c.pTa, c.pTb))

        if dbg_xhatT is not None:
            @c.block.sync
            def _(sync):
                c.wait(sync, "s0lnev")
                c.dbg_store(sync, "xhatT", c.xhatT[:])

        if stage >= 2:
            emit_ffn(c, 0, x_in, xr1 if stage >= 5 else y,
                     wq, wo_d, wo_m, r_w, cos_t, sin_t,
                     router_scr, thr_scr)
        if stage >= 5:
            emit_ln(c, 1, xr1, c.xhatT, (c.pTa, c.pTb), after=("s0ep", c.val("s0ep")))
            emit_attn(c, 1, xr1, xr2 if stage >= 6 else y,
                      w_inT, b_in, b_in_r, outwT, out_b, relpeT, shear_scr)
        if stage >= 6:
            emit_ln(c, 2, xr2, c.xhatT, (c.pTa, c.pTb), after=("s1ep", c.val("s1ep")))
            emit_ffn(c, 2, xr2, y, wq, wo_d, wo_m, r_w, cos_t, sin_t,
                     router_scr, thr_scr)

        if stage < 2:
            @c.block.sync
            def _(sync):
                c.inc(sync.dma_start(out=y, in_=x_in), "yst", 16)

        @c.block.sync
        def _(sync):
            for name in ("yst", "dbgst", "s0ep", "s1ep", "s2ep"):
                if name in c._sems:
                    c.wait(sync, name)
    return nc


def _host_inputs(inputs):
    f32c = lambda a: np.ascontiguousarray(np.asarray(a), dtype=np.float32)
    wiT = f32c(np.asarray(inputs["L0_attn_in_proj_weight"]).T)
    b_in_r = f32c(np.asarray(inputs["L0_attn_in_proj_bias"]).reshape(24, 128))
    b_in = f32c(b_in_r.T)
    outwT = f32c(np.asarray(inputs["L0_attn_out_w"]).T)
    out_b = f32c(np.asarray(inputs["L0_attn_out_b"]).reshape(1, D))
    relpeT = f32c(np.asarray(inputs["L0_attn_rel_pe"]).T)
    r_w = f32c(inputs["L0_ffn_mosa_router_w"])
    wqkv_d = np.asarray(inputs["L0_ffn_dense_wqkv"])
    wqkv_m = np.asarray(inputs["L0_ffn_mosa_wqkv"])
    w = {}
    for br, wq in (("d", wqkv_d), ("m", wqkv_m)):
        for i, m in enumerate(("q", "k", "v")):
            w[f"w_{br}{m}"] = f32c(wq[:, :, 64 * i:64 * (i + 1)].reshape(D, 512))
    wo_d = f32c(inputs["L0_ffn_dense_wo"])
    wo_m = f32c(inputs["L0_ffn_mosa_wo"])
    half = 16
    inv = (10000.0 ** (-np.arange(half, dtype=np.float64) / half))
    ang = np.arange(SEQ, dtype=np.float64)[:, None] * inv[None, :]
    cos_t = np.cos(ang).astype(ml_dtypes.bfloat16)
    sin_t = np.sin(ang).astype(ml_dtypes.bfloat16)
    ident = np.eye(128).astype(ml_dtypes.bfloat16)
    triu = np.triu(np.ones((128, 128))).astype(ml_dtypes.bfloat16)
    ones1 = np.ones((1, 128), np.float32)
    bdiag = np.kron(np.eye(8), np.ones((16, 16))).astype(ml_dtypes.bfloat16)

    x = np.asarray(inputs["L0_attn_q"], dtype=np.float32)
    shared = dict(w_inT=wiT, b_in=b_in, b_in_r=b_in_r, outwT=outwT, out_b=out_b, relpeT=relpeT,
                  r_w=r_w, wo_d=wo_d, wo_m=wo_m, cos_t=cos_t, sin_t=sin_t,
                  ident_in=ident, triu_in=triu, ones1_in=ones1, bdiag_in=bdiag, **w)
    in_maps = []
    for b in range(8):
        m = dict(shared)
        m["x_in"] = np.ascontiguousarray(x[b])
        in_maps.append(m)
    return in_maps


_CACHED = {}


def kernel(**inputs):
    in_maps = _host_inputs(inputs)
    key = ("prog", BUILD_STAGE, tuple(sorted(DBG)))
    if key not in _CACHED:
        _CACHED[key] = build_program(BUILD_STAGE)
    nc = _CACHED[key]
    res = run_bass_kernel_spmd(nc, in_maps, core_ids=list(range(8)))
    out = np.stack([np.asarray(r["y"]) for r in res.results])
    kernel.last_results = res
    return out


def emit_ffn(c, si, x_src, x_dst, wq_dram, wo_d_dram, wo_m_dram, r_w_dram,
             cos_dram, sin_dram, router_scr, thr_scr):
    nc = c.nc
    p = f"s{si}f"
    lnev = (f"s{si}ln" + "ev", c.val(f"s{si}ln" + "ev"))

    r_w_sb = c.shared("f_rw", [128, 64], BF16)
    cos_sb = c.shared("f_cos", [128, NT * 16], BF16)
    sin_sb = c.shared("f_sin", [128, NT * 16], BF16)
    _xta = c.shared("ln_xta", [128, 2 * D])
    scb = _xta[0:8, 1024:1536]
    snc = _xta[:, 0:256]
    cmp_t = _xta[:, 256:512]
    sig_t = _xta[:, 512:768]
    mask_t = _xta[:, 768:1024]
    bis = c.shared("f_bis", [128, 8])
    cpart = c.shared("f_cpart", [128, 1])
    cpartb = c.shared("f_cpartb", [128, 1], BF16)
    thr8 = c.shared("f_thr8", [1, 8])
    thr_rep = c.shared("f_threp", [128, 8])
    bias_all = c.shared("f_biasall", [128, 256])
    gate_all = c.shared("f_gateall", [128, 256])
    wbuf = c.shared("wbuf", [128, 8192], BF16)
    wpair = wbuf
    q_sb = c.shared("f_qsb", [128, NT * 128], BF16)
    k_sb = c.shared("f_ksb", [128, NT * 128], BF16)
    v_aug = c.shared("f_vaug", [128, NT * 130], BF16)
    xta_f32 = c.shared("ln_xta", [128, 2 * D])
    xtb_f32 = c.shared("ln_xtb", [128, 2 * D])
    rt_all = xta_f32[:].bitcast(BF16)
    rt = [rt_all[:, 1024 * i:1024 * (i + 1)] for i in range(4)]
    qT = c.shared("f_qT", [128, SEQ], BF16)
    kT = q_sb
    PT = c.shared("f_PT", [128, 2 * 512], BF16)
    recip4 = c.shared("f_recip", [128, 4])
    scale4 = c.shared("f_scale", [128, 4])
    wo_sb = wbuf
    ep_tmp = c.shared("ln_xhat", [128, 2 * D], BF16)[:].bitcast(F32)
    ep_x = xta_f32
    ep_y = xtb_f32
    ones1 = c.ones1
    bdiag = c.bdiag

    bout = c.big2

    @c.block.gpsimd
    def _(gpsimd):
        c.wait_role(gpsimd, "awm")
        c.inc(gpsimd.dma_start(
            out=r_w_sb[:].rearrange("p (h n) -> p h n", h=HCH),
            in_=r_w_dram.rearrange("(h p) n -> p h n", p=128)), p + "cwld", 16)
        c.inc(gpsimd.dma_start(
            out=cos_sb[:].rearrange("p (t k) -> p t k", t=NT),
            in_=cos_dram.rearrange("(t p) k -> p t k", p=128)), p + "cwld", 16)
        c.inc(gpsimd.dma_start(
            out=sin_sb[:].rearrange("p (t k) -> p t k", t=NT),
            in_=sin_dram.rearrange("(t p) k -> p t k", p=128)), p + "cwld", 16)
        _ffn_pair_weights(c, p, gpsimd, wq_dram, wpair, 0, 0)

    @c.block.vector
    def _(vector):
        va = AP(v_aug, 64, [[NT * 130, 128], [130, NT], [65, 2]])
        c.inc(nc.vector.memset(va, 1.0), p + "vones", 1)

    @c.block.tensor
    def _(tensor):
        c.wait(tensor, f"s{si}ln" + "ev")
        c.wait(tensor, p + "cwld")
        for f in range(HCH):
            for h in range(HCH):
                ins = nc.tensor.matmul(c.pC[0:8, :], r_w_sb[:, 8 * h:8 * h + 8],
                                       c.xhatT[:, SEQ * h + 512 * f:SEQ * h + 512 * (f + 1)],
                                       start=(h == 0), stop=(h == HCH - 1))
            c.inc(ins, p + "scm", 1)
            if f < HCH - 1:
                c.wait(tensor, p + "scev", f + 1)
        for t in range(NT):
            for h in range(HCH):
                ins = nc.tensor.matmul(c.pD[:, 8 * t:8 * t + 8],
                                       c.xhatT[:, SEQ * h + 128 * t:SEQ * h + 128 * (t + 1)],
                                       r_w_sb[:, 8 * h:8 * h + 8],
                                       start=(t == 0 and h == 0),
                                       stop=(t == NT - 1 and h == HCH - 1))
        c.inc(ins, p + "stm", 1)

    @c.block.vector
    def _(vector):
        for f in range(HCH):
            c.wait(vector, p + "scm", c.val(p + "scm") - HCH + f + 1)
            if f >= 1:
                c.wait(vector, p + "scwr", 16 * f)
            c.inc(nc.vector.tensor_copy(scb, c.pC[0:8, :]), p + "scev", 1)

    @c.block.sync
    def _(sync):
        for f in range(HCH):
            c.wait(sync, p + "scev", c.val(p + "scev") - HCH + f + 1)
            c.inc(sync.dma_start(out=router_scr.ap()[:, 512 * f:512 * (f + 1)], in_=scb),
                  p + "scwr", 16)
        c.wait(sync, p + "scwr")
        c.inc(sync.dma_start(out=snc, in_=router_scr.ap().rearrange("n (c w) -> (n c) w", c=16)),
              p + "scrd", 16)

    LO, HI, THR, STEP, G, GS, T2, CNT = range(8)

    @c.block.vector
    def _(vector):
        c.wait(vector, p + "scrd")
        nc.vector.memset(bis[:, LO:LO + 1], -24.0)
        nc.vector.memset(bis[:, HI:HI + 1], 24.0)
        nc.vector.drain()

    for it in range(30):
        @c.block.vector
        def _(vector):
            nc.vector.tensor_tensor(bis[:, STEP:STEP + 1], bis[:, HI:HI + 1],
                                    bis[:, LO:LO + 1], ALU.subtract)
            nc.vector.drain()
            nc.vector.tensor_scalar_mul(bis[:, STEP:STEP + 1], bis[:, STEP:STEP + 1], 0.5)
            nc.vector.drain()
            nc.vector.tensor_tensor(bis[:, THR:THR + 1], bis[:, LO:LO + 1],
                                    bis[:, STEP:STEP + 1], ALU.add)
            nc.vector.drain()
            nc.vector.tensor_scalar(cmp_t, snc, bis[:, THR:THR + 1], None, ALU.is_gt)
            nc.vector.drain()
            nc.vector.reduce_sum(cpart[:], cmp_t, axis=AX.X)
            nc.vector.drain()
            c.inc(nc.vector.tensor_copy(cpartb[:], cpart[:]), p + "bv", 1)

        @c.block.tensor
        def _(tensor):
            c.wait(tensor, p + "bv")
            if it >= 1:
                c.wait(tensor, p + "bu", it)
            c.inc(nc.tensor.matmul(c.pC[:, 508 + (it % 4):509 + (it % 4)], bdiag[:], cpartb[:],
                                   start=True, stop=True), p + "bt", 1)

        @c.block.vector
        def _(vector):
            c.wait(vector, p + "bt")
            nc.vector.tensor_copy(bis[:, CNT:CNT + 1], c.pC[:, 508 + (it % 4):509 + (it % 4)])
            nc.vector.drain()
            c.inc(nc.vector.nop(), p + "bu", 1)
            nc.vector.tensor_scalar(bis[:, G:G + 1], bis[:, CNT:CNT + 1], 512.0, None, ALU.is_gt)
            nc.vector.drain()
            nc.vector.tensor_tensor(bis[:, GS:GS + 1], bis[:, G:G + 1],
                                    bis[:, STEP:STEP + 1], ALU.mult)
            nc.vector.drain()
            nc.vector.tensor_tensor(bis[:, LO:LO + 1], bis[:, LO:LO + 1],
                                    bis[:, GS:GS + 1], ALU.add)
            nc.vector.tensor_tensor(bis[:, T2:T2 + 1], bis[:, STEP:STEP + 1],
                                    bis[:, GS:GS + 1], ALU.subtract)
            nc.vector.drain()
            nc.vector.tensor_tensor(bis[:, HI:HI + 1], bis[:, HI:HI + 1],
                                    bis[:, T2:T2 + 1], ALU.subtract)
            c.inc(nc.vector.drain(), p + "bi", 1)

    @c.block.sync
    def _(sync):
        c.wait(sync, p + "bi")
        c.inc(sync.dma_start(out=thr_scr.ap(), in_=bis[:, HI:HI + 1]), p + "thw", 16)
        c.wait(sync, p + "thw")
        c.inc(sync.dma_start(out=thr8[:], in_=AP(thr_scr, 0, [[1, 1], [16, 8]]),
                             allow_slow_non_contiguous=True), p + "thr", 16)

    @c.block.tensor
    def _(tensor):
        c.wait(tensor, p + "thr")
        c.inc(nc.tensor.matmul(c.pC[:, 496:504], ones1[:], thr8[:], start=True, stop=True),
              p + "thm", 1)

    @c.block.scalar
    def _(scalar):
        c.wait(scalar, p + "stm")
        c.inc(nc.scalar.activation(sig_t, c.pD[:, 0:256], AF.Sigmoid), p + "sg", 1)

    @c.block.vector
    def _(vector):
        c.wait(vector, p + "thm")
        nc.vector.tensor_copy(thr_rep[:], c.pC[:, 496:504])
        nc.vector.drain()
        c.wait(vector, p + "sg")
        nc.vector.tensor_tensor(mask_t.rearrange("q (t n) -> q t n", n=8),
                                c.pD[:, 0:256].rearrange("q (t n) -> q t n", n=8),
                                thr_rep[:].unsqueeze(1).broadcast_to([128, NT, 8]),
                                ALU.is_gt)
        nc.vector.drain()
        nc.vector.tensor_scalar(bias_all[:], mask_t, -1.0, 1.0e9, ALU.add, ALU.mult)
        nc.vector.tensor_tensor(gate_all[:], mask_t, sig_t, ALU.mult)
        c.inc(nc.vector.drain(), p + "msk", 1)

    dbg_rt = c.dbg(f"s{si}router", [128, 3 * 256])
    if dbg_rt is not None:
        @c.block.sync
        def _(sync):
            c.wait(sync, p + "msk")
            c.inc(sync.dma_start(out=dbg_rt[:, 0:256], in_=bias_all[:]), "dbgst", 16)
            c.inc(sync.dma_start(out=dbg_rt[:, 256:512], in_=gate_all[:]), "dbgst", 16)
            c.inc(sync.dma_start(out=dbg_rt[:, 512:520], in_=thr_rep[:]), "dbgst", 16)

    for itn in range(8):
        br, pr = itn // 4, itn % 4
        _ffn_pair(c, si, p, itn, br, pr, wq_dram, wpair, q_sb, k_sb, v_aug, rt,
                  qT, kT, PT, recip4, scale4, bias_all, gate_all, bout,
                  cos_sb, sin_sb)

    _ffn_wo_epilogue(c, si, p, x_src, x_dst, bout, wo_sb, ep_tmp, ep_x, ep_y,
                     wo_d_dram, wo_m_dram)


def _ffn_pair_weights(c, p, gpsimd, wq_dram, wpair, itn, buf):
    br, pr = itn // 4, itn % 4
    names = ("w_dq", "w_dk", "w_dv") if br == 0 else ("w_mq", "w_mk", "w_mv")
    for mi, nm in enumerate(names):
        src = wq_dram[nm][:, 128 * pr:128 * (pr + 1)]
        dst = wpair[:, 3072 * buf + 1024 * mi:3072 * buf + 1024 * (mi + 1)
                    ].rearrange("p (h j) -> p h j", h=HCH)
        c.inc(gpsimd.dma_start(out=dst, in_=src.rearrange("(h p) j -> p h j", p=128)),
              p + "wld" + str(itn % 2), 16)


def _ffn_pair(c, si, p, itn, br, pr, wq_dram, wpair, q_sb, k_sb, v_aug, rt,
              qT, kT, PT, recip4, scale4, bias_all, gate_all, bout, cos_sb, sin_sb):
    nc = c.nc
    buf = itn % 2
    wld_needed = c.val(p + "wld" + str(itn % 2))

    if itn + 1 < 8:
        @c.block.gpsimd
        def _(gpsimd):
            if itn >= 1:
                c.wait(gpsimd, p + "qm", 32 * itn)
            _ffn_pair_weights(c, p, gpsimd, wq_dram, wpair, itn + 1, (itn + 1) % 2)

    qm0 = c.val(p + "qm")
    qe0 = c.val(p + "qe")
    avk0 = c.val(p + "avk")

    @c.block.tensor
    def _(tensor):
        c.wait(tensor, p + "wld" + str(itn % 2), wld_needed)
        if itn == 0:
            c.wait(tensor, p + "msk")
            c.wait(tensor, p + "vones")
        for t in range(NT):
            ps = (c.pW, c.pC)[t % 2]
            if t >= 2:
                c.wait(tensor, p + "qe", qe0 + t - 1)
            for h in range(HCH):
                for mi in range(3):
                    ins = nc.tensor.matmul(
                        ps[:, 128 * mi:128 * (mi + 1)],
                        c.xhatT[:, SEQ * h + 128 * t:SEQ * h + 128 * (t + 1)],
                        wpair[:, 3072 * buf + 1024 * mi + 128 * h:
                              3072 * buf + 1024 * mi + 128 * (h + 1)],
                        start=(h == 0 and mi == 0), stop=(h == HCH - 1 and mi == 2))
            c.inc(ins, p + "qm", 1)

    @c.block.vector
    def _(vector):
        if itn >= 1:
            c.wait(vector, p + "avk0", c.val(p + "avk0"))
            c.wait(vector, p + "avk1", c.val(p + "avk1"))
        for t in range(NT):
            c.wait(vector, p + "qm", qm0 + t + 1)
            ps = (c.pW, c.pC)[t % 2]
            nc.vector.tensor_copy(q_sb[:, 128 * t:128 * (t + 1)], ps[:, 0:128])
            nc.vector.tensor_copy(k_sb[:, 128 * t:128 * (t + 1)], ps[:, 128:256])
            c.inc(nc.vector.tensor_copy(
                AP(v_aug, 130 * t, [[NT * 130, 128], [65, 2], [1, 64]]),
                ps[:, 256:384].rearrange("q (g d) -> q g d", g=2)), p + "qe", 1)

    @c.block.vector
    def _(vector):
        c.wait(vector, p + "qe", qe0 + NT)
        for mat in (q_sb, k_sb):
            x1 = AP(mat, 0, [[NT * 128, 128], [128, NT], [64, 2], [1, 16]])
            x2 = AP(mat, 16, [[NT * 128, 128], [128, NT], [64, 2], [1, 16]])
            cosv = AP(cos_sb, 0, [[NT * 16, 128], [16, NT], [0, 2], [1, 16]])
            sinv = AP(sin_sb, 0, [[NT * 16, 128], [16, NT], [0, 2], [1, 16]])
            sh = [128, NT, 2, 16]
            r0 = rt[0][:, 0:1024].rearrange("p (t g d) -> p t g d", t=NT, g=2)
            r1 = rt[1][:, 0:1024].rearrange("p (t g d) -> p t g d", t=NT, g=2)
            r2 = rt[2][:, 0:1024].rearrange("p (t g d) -> p t g d", t=NT, g=2)
            r3 = rt[3][:, 0:1024].rearrange("p (t g d) -> p t g d", t=NT, g=2)
            nc.vector.tensor_tensor(r0, x1, cosv, ALU.mult)
            nc.vector.tensor_tensor(r1, x2, sinv, ALU.mult)
            nc.vector.tensor_tensor(r2, x1, sinv, ALU.mult)
            nc.vector.tensor_tensor(r3, x2, cosv, ALU.mult)
            nc.vector.drain()
            nc.vector.tensor_tensor(x1, r0, r1, ALU.subtract)
            nc.vector.tensor_tensor(x2, r2, r3, ALU.add)
            nc.vector.drain()
        c.inc(nc.vector.nop(), p + "rp", 1)

    rp_need = c.val(p + "rp")
    tq0 = c.val(p + "tq")
    te0 = c.val(p + "te")

    @c.block.tensor
    def _(tensor):
        c.wait(tensor, p + "rp", rp_need)
        g = 0
        for mi, mat in enumerate((q_sb, k_sb)):
            for t in range(NT):
                if t % 8 == 0:
                    if g >= 1:
                        c.wait(tensor, p + "te", te0 + g)
                    pT = (c.pTa, c.pTb)[g % 2]
                ins = nc.tensor.transpose(pT[:, 128 * (t % 8):128 * (t % 8 + 1)],
                                          mat[:, 128 * t:128 * (t + 1)], c.ident[:])
                if t % 8 == 7:
                    c.inc(ins, p + "tq", 1)
                    g += 1

    @c.block.vector
    def _(vector):
        for g in range(8):
            mi, tg = g // 4, g % 4
            c.wait(vector, p + "tq", tq0 + g + 1)
            dstT = (qT, kT)[mi]
            c.inc(nc.vector.tensor_copy(
                AP(dstT, 1024 * tg, [[SEQ, 128], [1, 1024]]),
                (c.pTa, c.pTb)[g % 2][:]), p + "te", 1)

    te_need = c.val(p + "te")

    _ffn_head(c, si, p, itn, br, pr, te_need, qT, kT, v_aug, PT,
              recip4, scale4, bias_all, gate_all, bout)


def _ffn_head(c, si, p, itn, br, pr, te_need, qT, kT, v_aug, PT, recip4,
              scale4, bias_all, gate_all, bout):
    nc = c.nc
    wbuf = c.shared("wbuf", [128, 8192], BF16)

    def pts(h):
        return (PT[:, 0:1024], wbuf[:, 6144:7168])[h]

    NQB = 16
    blocks = []
    for Qb in range(NQB):
        full = list(range(2 * Qb))
        sups = []
        i = 0
        while i < len(full):
            if br == 0:
                sups.append([(full[i + k], 256, 0) for k in range(min(4, len(full) - i))])
                i += len(sups[-1])
            else:
                sups.append([(full[i], 256, 0)])
                i += 1
        for m in range(2):
            sups.append([(2 * Qb + m, 256 - 128 * m, 128 * m)])
        blocks.append((Qb, sups))

    ex0 = [c.val(p + "ex0"), c.val(p + "ex1")]
    sm0 = [c.val(p + "sm0"), c.val(p + "sm1")]
    avk0 = [c.val(p + "avk0"), c.val(p + "avk1")]
    dm0 = c.val(p + "dm")
    nrm_before = c.val(p + "nrm")
    ACO = (0, 260)
    dmap = {}
    nd = dm0
    g2 = 0
    for (Qb, sups) in blocks:
        for sup in sups:
            if sup[0][0] >= 2 * Qb:
                dmap[(g2, 0)] = nd + 1
                dmap[(g2, 1)] = nd + 2
                nd += 2
            g2 += 1

    @c.block.tensor
    def _(tensor):
        c.wait(tensor, p + "te", te_need)
        g2 = 0
        started_blk = [False]
        blk_last = {}
        gg = 0
        for (Qb, sups) in blocks:
            gg += len(sups)
            blk_last[Qb] = gg - 1

        def emit_S(h, g, sup, Qb):
            psS = (c.pSa, c.pSb)[h]
            c.wait(tensor, p + "ex" + str(h), ex0[h] + g)
            for half, (kt, ncols, coff) in enumerate(sup):
                c.inc(nc.tensor.matmul(
                    psS[:, 256 * half:256 * half + ncols],
                    kT[64 * h:64 * (h + 1), 128 * kt:128 * (kt + 1)],
                    qT[64 * h:64 * (h + 1), 256 * Qb + coff:256 * (Qb + 1)],
                    start=True, stop=True), p + "sm" + str(h), 1)

        def emit_av(h, g, sup, Qb):
            diag = sup[0][0] >= 2 * Qb
            if diag:
                c.wait(tensor, p + "dm", dmap[(g, h)])
            else:
                c.wait(tensor, p + "ex" + str(h), ex0[h] + g + 1)
            for half, (kt, ncols, coff) in enumerate(sup):
                m = coff // 128
                for j in range(m, 2):
                    ins = nc.tensor.matmul(
                        c.pACC[:, ACO[h] + 65 * j:ACO[h] + 65 * j + 65],
                        pts(h)[:, 256 * half + 128 * j - coff:
                               256 * half + 128 * j - coff + 128],
                        v_aug[:, 130 * kt + 65 * h:130 * kt + 65 * h + 65],
                        start=(not started_blk[0]),
                        stop=(g == blk_last[Qb] and h == 1 and half == len(sup) - 1
                              and j == 1))
                    started_blk[0] = True
            c.inc(ins, p + "avk" + str(h), 1)

        for (Qb, sups) in blocks:
            started_blk[0] = False
            c.wait(tensor, p + "nrm", nrm_before + Qb)
            prev = None
            for sup in sups:
                emit_S(0, g2, sup, Qb)
                emit_S(1, g2, sup, Qb)
                if prev is not None:
                    emit_av(0, *prev)
                    emit_av(1, *prev)
                prev = (g2, sup, Qb)
                g2 += 1
            emit_av(0, *prev)
            emit_av(1, *prev)

    @c.block.scalar
    def _(scalar):
        g2 = 0
        nsm = [sm0[0], sm0[1]]
        for (Qb, sups) in blocks:
            for sup in sups:
                for h in range(2):
                    nsm[h] += len(sup)
                    c.wait(scalar, p + "sm" + str(h), nsm[h])
                    c.wait(scalar, p + "avk" + str(h), avk0[h] + g2)
                    psS = (c.pSa, c.pSb)[h]
                    fd = 256 * (len(sup) - 1) + sup[-1][1]
                    kw = {"scale": 0.125}
                    if br == 1:
                        hd = 2 * pr + h
                        kw["bias"] = bias_all[:, 8 * sup[0][0] + hd:8 * sup[0][0] + hd + 1]
                    c.inc(nc.scalar.activation(pts(h)[:, 0:fd], psS[:, 0:fd],
                                               AF.Exp, **kw), p + "ex" + str(h), 1)
                g2 += 1

    @c.block.vector
    def _(vector):
        g2 = 0
        ndone = 0
        for (Qb, sups) in blocks:
            for sup in sups:
                diag = sup[0][0] >= 2 * Qb
                if diag:
                    for h in range(2):
                        c.wait(vector, p + "ex" + str(h), ex0[h] + g2 + 1)
                        nc.vector.tensor_tensor(pts(h)[:, 0:128], pts(h)[:, 0:128],
                                                c.triu[:], ALU.mult)
                        c.inc(nc.vector.drain(), p + "dm", 1)
                g2 += 1
            ndone += len(sups)
            c.wait(vector, p + "avk0", avk0[0] + ndone)
            c.wait(vector, p + "avk1", avk0[1] + ndone)
            for h in range(2):
                hd = 2 * pr + h
                nc.vector.tensor_scalar(recip4[:, 0:2],
                                        AP(c.pACC, ACO[h] + 64, [[512, 128], [65, 2]]),
                                        1e-30, None, ALU.add)
                nc.vector.drain()
                nc.vector.reciprocal(recip4[:, 0:2], recip4[:, 0:2])
                nc.vector.drain()
                if br == 1:
                    nc.vector.tensor_tensor(scale4[:, 0:2], recip4[:, 0:2],
                                            AP(gate_all, 8 * 2 * Qb + hd, [[256, 128], [8, 2]]),
                                            ALU.mult)
                    nc.vector.drain()
                    sc = scale4
                else:
                    sc = recip4
                dst = AP(bout, br * (NT * 512) + 512 * 2 * Qb + 64 * hd,
                         [[NT * 1024, 128], [512, 2], [1, 64]])
                src = AP(c.pACC, ACO[h], [[512, 128], [65, 2], [1, 64]])
                nc.vector.tensor_tensor(dst, src,
                                        sc[:, 0:2].unsqueeze(2).broadcast_to([128, 2, 64]),
                                        ALU.mult)
                nc.vector.drain()
            c.inc(nc.vector.nop(), p + "nrm", 1)


def _ffn_wo_epilogue(c, si, p, x_src, x_dst, bout, wo_sb, ep_tmp, ep_x, ep_y,
                     wo_d_dram, wo_m_dram):
    nc = c.nc
    boutT = c.xhatT
    qm_total = c.val(p + "qm")

    @c.block.gpsimd
    def _(gpsimd):
        c.wait(gpsimd, p + "qm", qm_total)
        for h in range(2):
            c.wait(gpsimd, p + "ex" + str(h), c.val(p + "ex" + str(h)))
            c.wait(gpsimd, p + "avk" + str(h), c.val(p + "avk" + str(h)))
        for br in range(2):
            c.inc(gpsimd.dma_start(
                out=wo_sb[:, br * 4096:(br + 1) * 4096].rearrange("p (cc j) -> p cc j", cc=4),
                in_=(wo_d_dram, wo_m_dram)[br].rearrange("(cc p) j -> p cc j", p=128)),
                p + "wol", 16)
    nrm_total = c.val(p + "nrm")
    bt0 = c.val(p + "bt2")
    be0 = c.val(p + "be")
    wm0 = c.val(p + "wm")
    eps0 = c.val(p + "eps")
    epl0 = [c.val(p + "epl0"), c.val(p + "epl1")]

    dbg_bout = c.dbg(f"s{si}bout", [128, NT * 1024], BF16)
    if dbg_bout is not None:
        @c.block.sync
        def _(sync):
            c.wait(sync, p + "nrm", nrm_total)
            c.inc(sync.dma_start(out=dbg_bout, in_=bout[:]), "dbgst", 16)

    @c.block.tensor
    def _(tensor):
        c.wait(tensor, p + "nrm", nrm_total)
        for t in range(NT):
            if t >= 1:
                c.wait(tensor, p + "be", be0 + t)
            pT = (c.pTa, c.pTb)[t % 2]
            for ch in range(8):
                brx, cc = ch // 4, ch % 4
                ins = nc.tensor.transpose(
                    pT[:, 128 * ch:128 * (ch + 1)],
                    bout[:, brx * (NT * 512) + 512 * t + 128 * cc:
                         brx * (NT * 512) + 512 * t + 128 * (cc + 1)],
                    c.ident[:])
            c.inc(ins, p + "bt2", 1)

    @c.block.vector
    def _(vector):
        c.wait(vector, p + "qm", qm_total)
        for t in range(NT):
            c.wait(vector, p + "bt2", bt0 + t + 1)
            dst = AP(boutT, 128 * t, [[HCH * SEQ, 128], [SEQ, 8], [1, 128]])
            c.inc(nc.vector.tensor_copy(dst, (c.pTa, c.pTb)[t % 2][:].rearrange(
                "q (ch t) -> q ch t", ch=8)), p + "be", 1)

    @c.block.tensor
    def _(tensor):
        c.wait(tensor, p + "wol", 32)
        for t in range(NT):
            c.wait(tensor, p + "be", be0 + t + 1)
            if t >= 2:
                c.wait(tensor, p + "eps", eps0 + t - 1)
            for e in range(2):
                ps = ((c.pSa[:, 0:512], c.pSa[:, 512:1024]) if t % 2 == 0
                      else (c.pSb[:, 0:512], c.pSb[:, 512:1024]))[e]
                first = True
                for brx in range(2):
                    for cc in range(4):
                        ins = nc.tensor.matmul(
                            ps[:],
                            boutT[:, (4 * brx + cc) * SEQ + 128 * t:
                                  (4 * brx + cc) * SEQ + 128 * (t + 1)],
                            wo_sb[:, 4096 * brx + 1024 * cc + 512 * e:
                                  4096 * brx + 1024 * cc + 512 * (e + 1)],
                            start=first, stop=(brx == 1 and cc == 3))
                        first = False
            c.inc(ins, p + "wm", 1)

    @c.block.sync
    def _(sync):
        c.wait(sync, p + "rp")
        for t in range(NT + 2):
            if t >= 2:
                c.wait(sync, p + "epv", t - 1)
            if t < NT:
                c.inc(sync.dma_start(out=ep_x[:, 1024 * (t % 2):1024 * (t % 2 + 1)],
                                     in_=x_src[128 * t:128 * (t + 1), :]),
                      p + "epl" + str(t % 2), 16)
            if t >= 2:
                ts = t - 2
                c.inc(sync.dma_start(out=x_dst[128 * ts:128 * (ts + 1), :],
                                     in_=ep_y[:, 1024 * (ts % 2):1024 * (ts % 2 + 1)]),
                      p + "epw" + str(ts % 2), 16)
        c.wait(sync, p + "epw0")
        c.wait(sync, p + "epw1")
        c.inc(sync.nop(), f"s{si}ep", 1)

    @c.block.vector
    def _(vector):
        for t in range(NT):
            c.wait(vector, p + "wm", wm0 + t + 1)
            psf = (c.pSa, c.pSb)[t % 2]
            nc.vector.drain()
            c.inc(nc.vector.tensor_scalar_mul(ep_tmp[:], psf[:], 0.5), p + "eps", 1)
            c.wait(vector, p + "epl" + str(t % 2), epl0[t % 2] + 16 * (t // 2 + 1))
            if t >= 2:
                c.wait(vector, p + "epw" + str(t % 2), 16 * ((t - 2) // 2 + 1))
            nc.vector.drain()
            c.inc(nc.vector.tensor_tensor(ep_y[:, 1024 * (t % 2):1024 * (t % 2 + 1)],
                                          ep_tmp[:], ep_x[:, 1024 * (t % 2):1024 * (t % 2 + 1)],
                                          ALU.add), p + "epv", 1)





def emit_attn(c, si, x_src, x_dst, w_inT, b_in_dram, b_in_r_dram, outwT, out_b_dram,
              relpeT_dram, shear_scr):
    nc = c.nc
    p = f"s{si}a"
    lnev_name = f"s{si}ln" + "ev"

    wbuf = c.shared("wbuf", [128, 8192], BF16)
    qT = c.shared("f_qT", [128, SEQ], BF16)
    klT = c.shared("f_qsb", [128, NT * 128], BF16)
    krT = c.shared("f_ksb", [128, NT * 128], BF16)
    v_aug = c.shared("f_vaug", [128, NT * 130], BF16)
    PT = c.shared("f_PT", [128, 2 * 512], BF16)
    recip4 = c.shared("f_recip", [128, 4])
    b_in_sb = c.sb(p + "bin", [128, 24])
    relpeT_sb = c.sb(p + "relpe", [128, 132], BF16)
    rp01 = c.sb(p + "rp01", [128, 2])
    kbias = c.sb(p + "kbias", [128, 2])
    bvrow = c.sb(p + "bvrow", [1, 128])
    bv_rep = c.sb(p + "bvrep", [128, 2 * 128])
    ep_x = c.shared("ln_xta", [128, 2 * D])
    ep_y = c.shared("ln_xtb", [128, 2 * D])
    e_sb = ep_x[:, 0:640]
    w_sb = ep_x[:, 640:1408]
    br_sb = ep_x[:, 1408:1666]

    _xtbb = ep_y[:].bitcast(BF16)
    ptw = _xtbb[:, 0:1536]
    ptT = _xtbb[:, 1536:2304]
    wexp = _xtbb[:, 2304:3072]
    obrow = ep_y[0:1, 1536:2048]
    outb_rep = v_aug[:].bitcast(F32)[:, 0:1024]
    ep_tmp = c.shared("ln_xhat", [128, 2 * D], BF16)[:].bitcast(F32)
    aout = c.big2

    @c.block.gpsimd
    def _(gpsimd):
        c.wait_role(gpsimd, "fwm")
        c.inc(gpsimd.dma_start(out=b_in_sb[:], in_=b_in_dram), p + "cw", 16)
        c.inc(gpsimd.dma_start(out=relpeT_sb[0:64, 0:129], in_=relpeT_dram), p + "cw", 16)
        c.inc(gpsimd.dma_start(out=relpeT_sb[64:128, 0:129], in_=relpeT_dram), p + "cw", 16)
        for half in range(2):
            c.inc(gpsimd.dma_start(out=rp01[64 * half:64 * (half + 1), 0:1],
                                   in_=relpeT_dram[:, 0:1],
                                   allow_slow_non_contiguous=True), p + "cw", 16)
            c.inc(gpsimd.dma_start(out=rp01[64 * half:64 * (half + 1), 1:2],
                                   in_=relpeT_dram[:, 128:129],
                                   allow_slow_non_contiguous=True), p + "cw", 16)
        _attn_pair_weights(c, p, gpsimd, w_inT, wbuf, 0)

    @c.block.vector
    def _(vector):
        c.wait(vector, p + "cw", 16 * 7)
        c.wait_role(vector, "lnst")
        nc.vector.memset(w_sb[:, 0:192], 0.0)
        nc.vector.drain()
        c.inc(nc.vector.nop(), p + "rp01", 1)

    for pr in range(8):
        _attn_pair(c, si, p, pr, w_inT, wbuf, b_in_sb, rp01, kbias, bvrow, bv_rep,
                   b_in_r_dram, qT, klT, krT, v_aug, PT, recip4, relpeT_sb,
                   br_sb, e_sb, w_sb, ptw, ptT, wexp, shear_scr, aout)

    _attn_proj_epilogue(c, si, p, x_src, x_dst, aout, wbuf, outwT, outb_rep,
                        ep_tmp, ep_x, ep_y, obrow, out_b_dram)


def _attn_pair_weights(c, p, gpsimd, w_inT, wbuf, pr):
    buf = pr % 2
    for mi in range(3):
        src = w_inT[:, 1024 * mi + 128 * pr:1024 * mi + 128 * (pr + 1)]
        dst = wbuf[:, 3072 * buf + 1024 * mi:3072 * buf + 1024 * (mi + 1)
                   ].rearrange("p (h j) -> p h j", h=HCH)
        c.inc(gpsimd.dma_start(out=dst, in_=src.rearrange("(h p) j -> p h j", p=128)),
              p + "wld" + str(pr % 2), 16)


def _attn_pair(c, si, p, pr, w_inT, wbuf, b_in_sb, rp01, kbias, bvrow, bv_rep,
               b_in_r_dram, qT, klT, krT, v_aug, PT, recip4, relpeT_sb,
               br_sb, e_sb, w_sb, ptw, ptT, wexp, shear_scr, aout):
    nc = c.nc
    buf = pr % 2
    wld_needed = c.val(p + "wld" + str(pr % 2))

    if pr + 1 < 8:
        @c.block.gpsimd
        def _(gpsimd):
            if pr >= 1:
                c.wait(gpsimd, p + "qm", 49 * pr)
            _attn_pair_weights(c, p, gpsimd, w_inT, wbuf, pr + 1)

    @c.block.vector
    def _(vector):
        if pr == 0:
            c.wait(vector, p + "rp01")
        if pr >= 1:
            c.wait(vector, p + "qev", c.val(p + "qev"))
        nc.vector.tensor_scalar(kbias[:, 0:1], rp01[:, 0:1], b_in_sb[:, 8 + pr:9 + pr],
                                None, ALU.add)
        nc.vector.tensor_scalar(kbias[:, 1:2], rp01[:, 1:2], b_in_sb[:, 8 + pr:9 + pr],
                                None, ALU.add)
        c.inc(nc.vector.drain(), p + "kb", 1)

    @c.block.sync
    def _(sync):
        if pr >= 1:
            c.wait(sync, p + "qm", 49 * (pr - 1) + 1)
        c.inc(sync.dma_start(out=bvrow[:], in_=b_in_r_dram[16 + pr:17 + pr, :]),
              p + "bvl" + str(pr % 2), 16)

    qm0 = c.val(p + "qm")
    qev0 = c.val(p + "qev")
    kb_need = c.val(p + "kb")

    @c.block.tensor
    def _(tensor):
        c.wait(tensor, p + "wld" + str(pr % 2), wld_needed)
        c.wait(tensor, p + "bvl" + str(pr % 2), c.val(p + "bvl" + str(pr % 2)))
        if pr == 0:
            c.wait(tensor, lnev_name_global(si))
        else:
            c.wait(tensor, p + "wex", c.val(p + "wex"))
            c.wait(tensor, p + "wbe", c.val(p + "wbe"))
        c.inc(nc.tensor.matmul(c.pC[:, 0:128], c.ones1[:], bvrow[:], start=True, stop=True),
              p + "qm", 1)
        for mi in range(2):
            for f in range(HCH):
                ps = (c.pW, c.pC)[f % 2]
                gi = mi * HCH + f
                if gi == 1:
                    c.wait(tensor, p + "abv", c.val(p + "abv") + 1)
                if gi >= 2:
                    c.wait(tensor, p + "qev", qev0 + gi - 1)
                for h in range(HCH):
                    ins = nc.tensor.matmul(
                        ps[:],
                        wbuf[:, 3072 * buf + 1024 * mi + 128 * h:
                             3072 * buf + 1024 * mi + 128 * (h + 1)],
                        c.xhatT[:, SEQ * h + 512 * f:SEQ * h + 512 * (f + 1)],
                        start=(h == 0), stop=(h == HCH - 1))
                c.inc(ins, p + "qm", 1)
        for t in range(NT):
            ps = (c.pW, c.pC)[t % 2]
            c.wait(tensor, p + "qev", qev0 + 15 + t)
            for h in range(HCH):
                ins = nc.tensor.matmul(
                    ps[:, 0:128],
                    c.xhatT[:, SEQ * h + 128 * t:SEQ * h + 128 * (t + 1)],
                    wbuf[:, 3072 * buf + 2048 + 128 * h:3072 * buf + 2048 + 128 * (h + 1)],
                    start=(h == 0), stop=(h == HCH - 1))
            c.inc(ins, p + "qm", 1)

    @c.block.vector
    def _(vector):
        c.wait(vector, p + "qm", qm0 + 1)
        if pr >= 1:
            c.wait(vector, p + "avk", c.val(p + "avk"))
        nc.vector.tensor_copy(bv_rep[:, 128 * buf:128 * (buf + 1)], c.pC[:, 0:128])
        c.inc(nc.vector.drain(), p + "abv", 1)
        c.wait(vector, p + "kb", kb_need)
        for mi in range(2):
            for f in range(HCH):
                ps = (c.pW, c.pC)[f % 2]
                gi = mi * HCH + f
                c.wait(vector, p + "qm", qm0 + 1 + gi + 1)
                if mi == 0:
                    c.inc(nc.vector.tensor_scalar(
                        qT[:, 512 * f:512 * (f + 1)], ps[:],
                        b_in_sb[:, pr:pr + 1], 0.125, ALU.add, ALU.mult), p + "qev", 1)
                else:
                    nc.vector.tensor_scalar(klT[:, 512 * f:512 * (f + 1)], ps[:],
                                            kbias[:, 0:1], None, ALU.add)
                    c.inc(nc.vector.tensor_scalar(krT[:, 512 * f:512 * (f + 1)], ps[:],
                                                  kbias[:, 1:2], None, ALU.add),
                          p + "qev", 1)
        for t in range(NT):
            ps = (c.pW, c.pC)[t % 2]
            c.wait(vector, p + "qm", qm0 + 17 + t + 1)
            dstv = AP(v_aug, 130 * t, [[NT * 130, 128], [65, 2], [1, 64]])
            c.inc(nc.vector.tensor_tensor(
                dstv, ps[:, 0:128].rearrange("q (g d) -> q g d", g=2),
                bv_rep[:, 128 * buf:128 * (buf + 1)].rearrange("q (g d) -> q g d", g=2),
                ALU.add), p + "qev", 1)

    qev_need = c.val(p + "qev")
    _attn_head(c, si, p, pr, qev_need, qT, klT, krT, v_aug, PT, recip4,
               relpeT_sb, br_sb, e_sb, w_sb, ptw, ptT, wexp, shear_scr, aout)


def lnev_name_global(si):
    return f"s{si}ln" + "ev"


def _attn_head(c, si, p, pr, qev_need, qT, klT, krT, v_aug, PT, recip4,
               relpeT_sb, br_sb, e_sb, w_sb, ptw, ptT, wexp, shear_scr, aout):
    nc = c.nc
    wbuf = c.shared("wbuf", [128, 8192], BF16)

    def pts(h):
        return (PT[:, 0:1024], wbuf[:, 6144:7168])[h]

    NQB = 16
    blocks = []
    for Qb in range(NQB):
        subs = []
        for kt in range(0, 2 * Qb - 1):
            subs.append((kt, 0))
        for kt in range(2 * Qb + 3, NT):
            subs.append((kt, 1))
        pieces = []
        for j in range(2):
            T = 2 * Qb + j
            for kk in range(max(2 * Qb - 1, 0), T - 1):
                pieces.append((kk, 0, j))
            for kk in range(T + 2, 2 * Qb + 3):
                if kk < NT:
                    pieces.append((kk, 1, j))
        sups = []
        i = 0
        while i < len(subs):
            sups.append([subs[i + k] for k in range(min(4, len(subs) - i))])
            i += 4
        sups += [[x] for x in pieces]
        wins = []
        for j in range(2):
            qt = 128 * (2 * Qb + j)
            wins.append((j, qt, max(0, qt - 128), min(SEQ, qt + 256)))
        blocks.append((Qb, sups, wins))

    ex0 = [c.val(p + "ex0"), c.val(p + "ex1")]
    sm0 = [c.val(p + "sm0"), c.val(p + "sm1")]
    avk0 = [c.val(p + "avk0"), c.val(p + "avk1")]
    nrm_before = c.val(p + "nrm")
    wi0 = c.val(p + "wsm")
    ACO = (0, 260)

    @c.block.tensor
    def _(tensor):
        c.wait(tensor, p + "qev", qev_need)
        g2 = 0
        wi = 0
        started_blk = [False]

        def emit_S(h, g, sup, Qb):
            psS = (c.pSa, c.pSb)[h]
            c.wait(tensor, p + "ex" + str(h), ex0[h] + g)
            for half, sub in enumerate(sup):
                kt, var = sub[0], sub[1]
                kTv = (klT, krT)[var]
                if len(sub) == 2:
                    rhs = qT[64 * h:64 * (h + 1), 256 * Qb:256 * (Qb + 1)]
                    nq = 256
                else:
                    j = sub[2]
                    rhs = qT[64 * h:64 * (h + 1),
                             256 * Qb + 128 * j:256 * Qb + 128 * (j + 1)]
                    nq = 128
                c.inc(nc.tensor.matmul(psS[:, 256 * half:256 * half + nq],
                                       kTv[64 * h:64 * (h + 1), 128 * kt:128 * (kt + 1)],
                                       rhs, start=True, stop=True), p + "sm" + str(h), 1)

        def emit_av(h, g, sup, Qb):
            c.wait(tensor, p + "ex" + str(h), ex0[h] + g + 1)
            for half, sub in enumerate(sup):
                kt = sub[0]
                jr = range(2) if len(sub) == 2 else [sub[2]]
                for j in jr:
                    pcol = 128 * j if len(sub) == 2 else 0
                    ins = nc.tensor.matmul(
                        c.pACC[:, ACO[h] + 65 * j:ACO[h] + 65 * j + 65],
                        pts(h)[:, 256 * half + pcol:256 * half + pcol + 128],
                        v_aug[:, 130 * kt + 65 * h:130 * kt + 65 * h + 65],
                        start=(not started_blk[0]), stop=False)
                    started_blk[0] = True
            c.inc(ins, p + "avk" + str(h), 1)

        for (Qb, sups, wins) in blocks:
            started_blk[0] = False
            c.wait(tensor, p + "nrm", nrm_before + Qb)

            def w_br(widx):
                (j, qt, ws, we) = wins[widx % 2]
                h = widx // 2
                gwi = wbase + widx
                psw = (c.pW, c.pC)[gwi % 2]
                c.wait(tensor, p + "wex", max(gwi - 1, 0))
                c.inc(nc.tensor.matmul(psw[:, 0:129],
                                       qT[64 * h:64 * (h + 1), qt:qt + 128],
                                       relpeT_sb[64 * h:64 * (h + 1), 0:129],
                                       start=True, stop=True), p + "wbr", 1)

            def w_swin(widx):
                (j, qt, ws, we) = wins[widx % 2]
                h = widx // 2
                gwi = wbase + widx
                psw = (c.pW, c.pC)[gwi % 2]
                c.wait(tensor, p + "wbe", gwi + 1)
                c.inc(nc.tensor.matmul(psw[:, 0:we - ws],
                                       qT[64 * h:64 * (h + 1), qt:qt + 128],
                                       klT[64 * h:64 * (h + 1), ws:we],
                                       start=True, stop=True), p + "wsm", 1)

            wbase = wi0 + wi
            hooks = {0: [(w_br, 0), (w_br, 1)], 1: [(w_swin, 0)], 2: [(w_swin, 1)],
                     3: [(w_br, 2)], 4: [(w_swin, 2)], 5: [(w_br, 3)], 6: [(w_swin, 3)]}
            nsup = len(sups)
            prev = None
            for si2, sup in enumerate(sups):
                for (fn, widx) in hooks.get(si2, []):
                    fn(widx)
                emit_S(0, g2, sup, Qb)
                emit_S(1, g2, sup, Qb)
                if prev is not None:
                    emit_av(0, *prev)
                    emit_av(1, *prev)
                prev = (g2, sup, Qb)
                g2 += 1
            if prev is not None:
                emit_av(0, *prev)
                emit_av(1, *prev)
            for i in range(nsup, 7):
                for (fn, widx) in hooks.get(i, []):
                    fn(widx)
            for widx in range(4):
                (j, qt, ws, we) = wins[widx % 2]
                h = widx // 2
                gwi = wbase + widx
                W = we - ws
                c.wait(tensor, p + "wtt", gwi + 1)
                c.wait(tensor, p + "wte", gwi)
                for b in range(W // 128):
                    ins = nc.tensor.transpose(
                        c.pT[:, 128 * b:128 * (b + 1)],
                        ptw[:, 384 * (gwi % 4) + 128 * b:384 * (gwi % 4) + 128 * (b + 1)],
                        c.ident[:])
                c.inc(ins, p + "wtp", 1)
                c.wait(tensor, p + "wte", gwi + 1)
                for b in range(W // 128):
                    vt = ws // 128 + b
                    ins = nc.tensor.matmul(
                        c.pACC[:, ACO[h] + 65 * j:ACO[h] + 65 * j + 65],
                        ptT[:, 384 * (gwi % 2) + 128 * b:384 * (gwi % 2) + 128 * (b + 1)],
                        v_aug[:, 130 * vt + 65 * h:130 * vt + 65 * h + 65],
                        start=False, stop=(widx == 3 and b == W // 128 - 1))
                c.inc(ins, p + "wav", 1)
            wi += 4

    @c.block.scalar
    def _(scalar):
        g2 = 0
        wi = 0
        nsm = [sm0[0], sm0[1]]
        for (Qb, sups, wins) in blocks:
            def w_exp(widx):
                (j, qt, ws, we) = wins[widx % 2]
                gwi = wbase + widx
                W = we - ws
                slot = gwi % 2
                c.wait(scalar, p + "wsr" + str(slot), 16 * (gwi // 2 + 1))
                c.wait(scalar, p + "wtt", max(gwi - 1, 0))
                c.inc(nc.scalar.activation(wexp[:, 384 * slot:384 * slot + W],
                                           w_sb[:, 384 * slot:384 * slot + W], AF.Exp),
                      p + "wxw", 1)
                c.wait(scalar, p + "wsm", gwi + 1)
                c.wait(scalar, p + "wtp", max(gwi - 3, 0))
                psw = (c.pW, c.pC)[slot]
                c.inc(nc.scalar.activation(ptw[:, 384 * (gwi % 4):384 * (gwi % 4) + W],
                                           psw[:, 0:W], AF.Exp), p + "wex", 1)

            wbase = wi0 + wi
            shooks = {3: 0, 4: 1, 5: 2, 6: 3}
            nsup = len(sups)
            for si2, sup in enumerate(sups):
                if si2 in shooks:
                    w_exp(shooks[si2])
                for h in range(2):
                    nsm[h] += len(sup)
                    c.wait(scalar, p + "sm" + str(h), nsm[h])
                    c.wait(scalar, p + "avk" + str(h), avk0[h] + g2)
                    psS = (c.pSa, c.pSb)[h]
                    fd = 256 * (len(sup) - 1) + (256 if len(sup[-1]) == 2 else 128)
                    c.inc(nc.scalar.activation(pts(h)[:, 0:fd], psS[:, 0:fd], AF.Exp),
                          p + "ex" + str(h), 1)
                g2 += 1
            for i in range(nsup, 7):
                if i in shooks:
                    w_exp(shooks[i])
            wi += 4

    @c.block.vector
    def _(vector):
        g2 = 0
        wi = 0
        ndone = 0
        for (Qb, sups, wins) in blocks:
            for widx in range(4):
                (j, qt, ws, we) = wins[widx % 2]
                gwi = wi0 + wi
                W = we - ws
                slot = gwi % 2
                c.wait(vector, p + "wbr", gwi + 1)
                psw = (c.pW, c.pC)[slot]
                c.inc(nc.vector.tensor_copy(br_sb[:, 129 * slot:129 * (slot + 1)],
                                            psw[:, 0:129]), p + "wbe", 1)
                if gwi >= 2:
                    c.wait(vector, p + "wsw" + str(slot), 16 * (gwi // 2))
                nc.vector.drain()
                brs = br_sb[:, 129 * slot:129 * (slot + 1)]
                es = e_sb[:, 320 * slot:320 * (slot + 1)]
                nc.vector.tensor_scalar(es[:, 0:129], brs, brs[:, 0:1], None, ALU.subtract)
                nc.vector.tensor_scalar(es[:, 129:130], brs[:, 128:129], brs[:, 0:1],
                                        None, ALU.subtract)
                nc.vector.drain()
                nc.vector.tensor_copy(es[:, 130:320],
                                      es[:, 129:130].broadcast_to([128, 1, 190]).squeeze(1))
                c.inc(nc.vector.drain(), p + "web", 1)
                wi += 1
            for widx in range(4):
                gwi = wi0 + wi - 4 + widx
                (j, qt, ws, we) = wins[widx % 2]
                W = we - ws
                slot = gwi % 2
                c.wait(vector, p + "wex", gwi + 1)
                c.wait(vector, p + "wxw", gwi + 1)
                nc.vector.drain()
                c.inc(nc.vector.tensor_tensor(ptw[:, 384 * (gwi % 4):384 * (gwi % 4) + W],
                                              ptw[:, 384 * (gwi % 4):384 * (gwi % 4) + W],
                                              wexp[:, 384 * slot:384 * slot + W],
                                              ALU.mult),
                      p + "wtt", 1)
            for widx in range(4):
                gwi = wi0 + wi - 4 + widx
                (j, qt, ws, we) = wins[widx % 2]
                W = we - ws
                c.wait(vector, p + "wtp", gwi + 1)
                c.wait(vector, p + "wav", max(gwi - 1, 0))
                c.inc(nc.vector.tensor_copy(ptT[:, 384 * (gwi % 2):384 * (gwi % 2) + W],
                                            c.pT[:, 0:W]), p + "wte", 1)
            ndone += len(sups)
            c.wait(vector, p + "avk0", avk0[0] + ndone)
            c.wait(vector, p + "avk1", avk0[1] + ndone)
            c.wait(vector, p + "wav", wi0 + wi)
            for h in range(2):
                hd = 2 * pr + h
                nc.vector.tensor_scalar(recip4[:, 0:2],
                                        AP(c.pACC, ACO[h] + 64, [[512, 128], [65, 2]]),
                                        1e-30, None, ALU.add)
                nc.vector.drain()
                nc.vector.reciprocal(recip4[:, 0:2], recip4[:, 0:2])
                nc.vector.drain()
                dst = AP(aout, 1024 * 2 * Qb + 64 * hd, [[NT * 1024, 128], [1024, 2], [1, 64]])
                src = AP(c.pACC, ACO[h], [[512, 128], [65, 2], [1, 64]])
                nc.vector.tensor_tensor(dst, src,
                                        recip4[:, 0:2].unsqueeze(2).broadcast_to([128, 2, 64]),
                                        ALU.mult)
                nc.vector.drain()
            c.inc(nc.vector.nop(), p + "nrm", 1)

    @c.block.sync
    def _(sync):
        if pr == 0:
            c.wait(sync, p + "rp01")
            for slot in range(2):
                c.inc(sync.dma_start(
                    out=AP(shear_scr, 512 * 128 * slot, [[512, 128], [1, 192]]),
                    in_=w_sb[:, 0:192]), p + "wz", 16)
            c.wait(sync, p + "wz", 32 * (pr + 1) if False else c.val(p + "wz"))
        wi = 0
        for (Qb, sups, wins) in blocks:
            for widx in range(4):
                (j, qt, ws, we) = wins[widx % 2]
                gwi = wi0 + wi
                W = we - ws
                slot = gwi % 2
                c.wait(sync, p + "web", gwi + 1)
                c.inc(sync.dma_start(
                    out=AP(shear_scr, 512 * 128 * slot + 192, [[512, 128], [1, 320]]),
                    in_=e_sb[:, 320 * slot:320 * (slot + 1)]),
                      p + "wsw" + str(slot), 16)
                c.wait(sync, p + "wsw" + str(slot), 16 * (gwi // 2 + 1))
                if gwi >= 2:
                    c.wait(sync, p + "wxw", gwi - 1)
                off = 256 if qt == 0 else 128
                src = AP(shear_scr, 512 * 128 * slot + off, [[511, 128], [1, W]])
                c.inc(sync.dma_start(out=w_sb[:, 384 * slot:384 * slot + W], in_=src),
                      p + "wsr" + str(slot), 16)
                wi += 1


def _attn_proj_epilogue(c, si, p, x_src, x_dst, aout, wbuf, outwT, outb_rep,
                        ep_tmp, ep_x, ep_y, obrow, out_b_dram):
    nc = c.nc
    aoutT = c.xhatT
    qm_total = c.val(p + "qm")
    nrm_tot0 = c.val(p + "nrm")

    @c.block.sync
    def _(sync):
        c.wait(sync, p + "nrm", nrm_tot0)
        c.inc(sync.dma_start(out=obrow, in_=out_b_dram[:, 0:512]), p + "obl", 16)
        c.wait(sync, p + "obm", 1)
        c.inc(sync.dma_start(out=obrow, in_=out_b_dram[:, 512:1024]), p + "obl", 16)

    @c.block.tensor
    def _(tensor):
        c.wait(tensor, p + "obl", 16)
        c.wait(tensor, p + "nrm", nrm_tot0)
        c.inc(nc.tensor.matmul(c.pW[:, 0:512], c.ones1[:], obrow, start=True, stop=True),
              p + "obm", 1)
        c.wait(tensor, p + "obl", 32)
        c.inc(nc.tensor.matmul(c.pACC[:, 0:512], c.ones1[:], obrow, start=True, stop=True),
              p + "obm", 1)

    @c.block.vector
    def _(vector):
        c.wait(vector, p + "obm", 1)
        nc.vector.tensor_copy(outb_rep[:, 0:512], c.pW[:, 0:512])
        c.wait(vector, p + "obm", 2)
        c.inc(nc.vector.tensor_copy(outb_rep[:, 512:1024], c.pACC[:, 0:512]), p + "obr", 1)
    qev_total = c.val(p + "qev")
    nrm_total = c.val(p + "nrm")
    bt0 = c.val(p + "bt2")
    be0 = c.val(p + "be")
    wm0 = c.val(p + "wm")
    eps0 = c.val(p + "eps")
    epl0 = [c.val(p + "epl0"), c.val(p + "epl1")]

    @c.block.gpsimd
    def _(gpsimd):
        c.wait(gpsimd, p + "qm", qm_total)
        for h in range(2):
            c.wait(gpsimd, p + "ex" + str(h), c.val(p + "ex" + str(h)))
            c.wait(gpsimd, p + "avk" + str(h), c.val(p + "avk" + str(h)))
        dst = wbuf[:, 0:8192].rearrange("p (cc j) -> p cc j", cc=8)
        c.inc(gpsimd.dma_start(out=dst, in_=outwT.rearrange("(cc p) j -> p cc j", p=128)),
              p + "wol", 16)

    @c.block.tensor
    def _(tensor):
        c.wait(tensor, p + "nrm", nrm_total)
        for t in range(NT):
            if t >= 1:
                c.wait(tensor, p + "be", be0 + t)
            pT = (c.pTa, c.pTb)[t % 2]
            for ch in range(8):
                ins = nc.tensor.transpose(
                    pT[:, 128 * ch:128 * (ch + 1)],
                    aout[:, 1024 * t + 128 * ch:1024 * t + 128 * (ch + 1)],
                    c.ident[:])
            c.inc(ins, p + "bt2", 1)

    @c.block.vector
    def _(vector):
        c.wait(vector, p + "qev", qev_total)
        for t in range(NT):
            c.wait(vector, p + "bt2", bt0 + t + 1)
            dst = AP(aoutT, 128 * t, [[HCH * SEQ, 128], [SEQ, 8], [1, 128]])
            c.inc(nc.vector.tensor_copy(dst, (c.pTa, c.pTb)[t % 2][:].rearrange(
                "q (ch t) -> q ch t", ch=8)), p + "be", 1)

    @c.block.tensor
    def _(tensor):
        c.wait(tensor, p + "wol", 16)
        for t in range(NT):
            c.wait(tensor, p + "be", be0 + t + 1)
            if t >= 2:
                c.wait(tensor, p + "eps", eps0 + t - 1)
            for e in range(2):
                ps = ((c.pSa[:, 0:512], c.pSa[:, 512:1024]) if t % 2 == 0
                      else (c.pSb[:, 0:512], c.pSb[:, 512:1024]))[e]
                for cc in range(8):
                    ins = nc.tensor.matmul(
                        ps[:],
                        aoutT[:, cc * SEQ + 128 * t:cc * SEQ + 128 * (t + 1)],
                        wbuf[:, 1024 * cc + 512 * e:1024 * cc + 512 * (e + 1)],
                        start=(cc == 0), stop=(cc == 7))
            c.inc(ins, p + "wm", 1)

    @c.block.sync
    def _(sync):
        c.wait(sync, p + "nrm", nrm_total)
        for t in range(NT + 2):
            if t >= 2:
                c.wait(sync, p + "epv", t - 1)
            if t < NT:
                c.inc(sync.dma_start(out=ep_x[:, 1024 * (t % 2):1024 * (t % 2 + 1)],
                                     in_=x_src[128 * t:128 * (t + 1), :]),
                      p + "epl" + str(t % 2), 16)
            if t >= 2:
                ts = t - 2
                c.inc(sync.dma_start(out=x_dst[128 * ts:128 * (ts + 1), :],
                                     in_=ep_y[:, 1024 * (ts % 2):1024 * (ts % 2 + 1)]),
                      p + "epw" + str(ts % 2), 16)
        c.wait(sync, p + "epw0")
        c.wait(sync, p + "epw1")
        c.inc(sync.nop(), f"s{si}ep", 1)

    @c.block.vector
    def _(vector):
        c.wait(vector, p + "obr")
        for t in range(NT):
            c.wait(vector, p + "wm", wm0 + t + 1)
            psf = (c.pSa, c.pSb)[t % 2]
            nc.vector.drain()
            c.inc(nc.vector.tensor_tensor(ep_tmp[:], psf[:], outb_rep[:], ALU.add),
                  p + "eps", 1)
            c.wait(vector, p + "epl" + str(t % 2), epl0[t % 2] + 16 * (t // 2 + 1))
            if t >= 2:
                c.wait(vector, p + "epw" + str(t % 2), 16 * ((t - 2) // 2 + 1))
            nc.vector.drain()
            c.inc(nc.vector.tensor_tensor(ep_y[:, 1024 * (t % 2):1024 * (t % 2 + 1)],
                                          ep_tmp[:], ep_x[:, 1024 * (t % 2):1024 * (t % 2 + 1)],
                                          ALU.add), p + "epv", 1)


# version 08 (final)
# speedup vs baseline: 1.2966x; 1.2966x over previous
import numpy as np
import ml_dtypes
import re
import contextlib
import concourse.bass as bass
import concourse.mybir as mybir
from concourse.bass import AP
from concourse.bass_utils import run_bass_kernel_spmd

F32 = mybir.dt.float32
BF16 = mybir.dt.bfloat16
U32 = mybir.dt.uint32
AF = mybir.ActivationFunctionType
ALU = mybir.AluOpType
AX = mybir.AxisListType

SEQ = 4096
D = 1024
NT = 32
HCH = 8
NEG = -1.0e9

DBG = set()
BUILD_STAGE = 99


class Kc:

    def __init__(self, nc, stack, block):
        self.nc = nc
        self.stack = stack
        self.block = block
        self._sems = {}
        self.dbg_tensors = {}

    def sb(self, name, shape, dt=F32):
        return self.stack.enter_context(self.nc.sbuf_tensor(name, shape, dt))

    def shared(self, name, shape, dt=F32):
        if not hasattr(self, "_shared"):
            self._shared = {}
        if name not in self._shared:
            self._shared[name] = (self.sb(name, shape, dt), tuple(shape), dt)
        t, sh, d = self._shared[name]
        assert sh == tuple(shape) and d == dt, f"shared {name} shape/dtype mismatch"
        return t

    def ps(self, name, shape, dt=F32):
        return self.stack.enter_context(self.nc.psum_tensor(name, shape, dt))

    def sem(self, name):
        if not hasattr(self, "_roles"):
            self._roles = {}
        if name not in self._sems:
            role = re.sub(r"^s\d+", "", name)
            if role not in self._roles:
                self._roles[role] = [self.stack.enter_context(self.nc.semaphore(role)), 0]
            ph = self._roles[role]
            self._sems[name] = [ph, ph[1], 0]
        return self._sems[name]

    def inc(self, instr, name, k=1):
        s = self.sem(name)
        instr.then_inc(s[0][0], k)
        s[0][1] += k
        s[2] += k
        return instr

    def val(self, name):
        return self.sem(name)[2]

    def wait(self, eng, name, v=None):
        s = self.sem(name)
        eng.wait_ge(s[0][0], s[1] + (s[2] if v is None else v))

    def wait_role(self, eng, role):
        if hasattr(self, "_roles") and role in self._roles:
            ph = self._roles[role]
            eng.wait_ge(ph[0], ph[1])

    def dbg(self, name, shape, dt=F32):
        if name not in DBG:
            return None
        t = self.nc.dram_tensor("dbg_" + name, shape, dt, kind="ExternalOutput").ap()
        self.dbg_tensors[name] = t
        return t

    def dbg_store(self, sync, name, src_ap):
        if name in self.dbg_tensors:
            self.inc(sync.dma_start(out=self.dbg_tensors[name], in_=src_ap), "dbgst", 16)


def emit_ln(c, si, x_src, xhatT, pT2, after=None):
    nc = c.nc
    p = f"s{si}ln"
    xta = c.shared("ln_xta", [128, 2 * D])
    xtb = c.shared("ln_xtb", [128, 2 * D])
    stats = c.shared("ln_xtb", [128, 2 * D])[:, 0:NT * 12]
    mv = c.shared("ln_mv", [128, NT * 2])
    varv = c.shared("ln_varv", [128, NT])
    rstd = c.shared("ln_rstd", [128, NT])
    nbm = c.shared("ln_nbias", [128, NT])
    xhat = c.shared("ln_xhat", [128, 2 * D], BF16)
    ident = c.ident

    ld0 = [c.val(p + "ld0"), c.val(p + "ld1")]

    @c.block.sync
    def _(sync):
        if after is not None:
            c.wait(sync, after[0], after[1])
        for t in range(NT):
            if t >= 2:
                c.wait(sync, p + "st", t - 1)
            c.inc(sync.dma_start(out=xta[:, (t % 2) * D:(t % 2 + 1) * D],
                                 in_=x_src[128 * t:128 * (t + 1), :]), p + "ld" + str(t % 2), 16)

    @c.block.vector
    def _(vector):
        for t in range(NT):
            c.wait(vector, p + "ld" + str(t % 2), ld0[t % 2] + 16 * (t // 2 + 1))
            xt = xta[:, (t % 2) * D:(t % 2 + 1) * D]
            nc.vector.bn_stats(stats[:, 12 * t:12 * t + 6], xt[:, 0:512])
            c.inc(nc.vector.bn_stats(stats[:, 12 * t + 6:12 * t + 12], xt[:, 512:1024]),
                  p + "st", 1)
        nc.vector.drain()
        for t in range(NT):
            nc.vector.bn_aggr(mv[:, 2 * t:2 * t + 2], stats[:, 12 * t:12 * t + 12])
        nc.vector.drain()
        mvv = mv[:].rearrange("q (t two) -> q t two", two=2)
        nc.vector.tensor_scalar_add(varv[:], mvv[:, :, 1:2].squeeze(2), 1e-5)
        c.inc(nc.vector.drain(), p + "mv", 1)

    @c.block.scalar
    def _(scalar):
        c.wait(scalar, p + "mv")
        nc.scalar.activation(varv[:], varv[:], AF.Ln)
        nc.scalar.drain()
        nc.scalar.activation(rstd[:], varv[:], AF.Exp, scale=-0.5)
        c.inc(nc.scalar.drain(), p + "rstd", 1)

    @c.block.vector
    def _(vector):
        c.wait(vector, p + "rstd")
        mvv = mv[:].rearrange("q (t two) -> q t two", two=2)
        nc.vector.tensor_scalar_mul(nbm[:], mvv[:, :, 0:1].squeeze(2), -1.0)
        c.inc(nc.vector.drain(), p + "nb", 1)

    ld2_0 = [c.val(p + "ld2_0"), c.val(p + "ld2_1")]

    @c.block.sync
    def _(sync):
        c.wait(sync, p + "mv")
        for t in range(NT):
            if t >= 2:
                c.wait(sync, p + "xh", t - 1)
            c.inc(sync.dma_start(out=xtb[:, (t % 2) * D:(t % 2 + 1) * D],
                                 in_=x_src[128 * t:128 * (t + 1), :]), p + "ld2_" + str(t % 2), 16)

    @c.block.vector
    def _(vector):
        c.wait(vector, p + "nb")

        def evac(t):
            c.wait(vector, p + "tr", t + 1)
            dst = AP(xhatT, 128 * t, [[HCH * SEQ, 128], [SEQ, HCH], [1, 128]])
            src = pT2[t % 2][:].rearrange("q (h t) -> q h t", h=HCH)
            c.inc(nc.vector.tensor_copy(dst, src), p + "ev", 1)

        for t in range(NT):
            c.wait(vector, p + "ld2_" + str(t % 2), ld2_0[t % 2] + 16 * (t // 2 + 1))
            if t >= 2:
                c.wait(vector, p + "tr", t - 1)
            c.inc(nc.vector.tensor_scalar(xhat[:, (t % 2) * D:(t % 2 + 1) * D],
                                          xtb[:, (t % 2) * D:(t % 2 + 1) * D],
                                          nbm[:, t:t + 1], rstd[:, t:t + 1],
                                          ALU.add, ALU.mult),
                  p + "xh", 1)
            if t >= 1:
                evac(t - 1)
        evac(NT - 1)

    @c.block.tensor
    def _(tensor):
        for t in range(NT):
            c.wait(tensor, p + "xh", t + 1)
            if t >= 1:
                c.wait(tensor, p + "ev", t)
            pT = pT2[t % 2]
            for h in range(HCH):
                ins = nc.tensor.transpose(pT[:, 128 * h:128 * (h + 1)],
                                          xhat[:, (t % 2) * D + 128 * h:(t % 2) * D + 128 * (h + 1)],
                                          ident[:])
            c.inc(ins, p + "tr", 1)



    return dict(nbias=nbm, rstd=rstd)



def build_program(stage=99):
    nc = bass.Bass("TRN2", target_bir_lowering=False)
    x_in = nc.dram_tensor("x_in", [SEQ, D], F32, kind="ExternalInput").ap()
    w_inT = nc.dram_tensor("w_inT", [D, 3 * D], F32, kind="ExternalInput").ap()
    b_in = nc.dram_tensor("b_in", [128, 24], F32, kind="ExternalInput").ap()
    b_in_r = nc.dram_tensor("b_in_r", [24, 128], F32, kind="ExternalInput").ap()
    outwT = nc.dram_tensor("outwT", [D, D], F32, kind="ExternalInput").ap()
    out_b = nc.dram_tensor("out_b", [1, D], F32, kind="ExternalInput").ap()
    relpeT = nc.dram_tensor("relpeT", [64, 129], F32, kind="ExternalInput").ap()
    r_w = nc.dram_tensor("r_w", [D, 8], F32, kind="ExternalInput").ap()
    wq = {}
    for br in ("d", "m"):
        for m in ("q", "k", "v"):
            wq[f"w_{br}{m}"] = nc.dram_tensor(f"w_{br}{m}", [D, 512], F32, kind="ExternalInput").ap()
    wo_d = nc.dram_tensor("wo_d", [512, D], F32, kind="ExternalInput").ap()
    wo_m = nc.dram_tensor("wo_m", [512, D], F32, kind="ExternalInput").ap()
    cos_t = nc.dram_tensor("cos_t", [SEQ, 16], BF16, kind="ExternalInput").ap()
    sin_t = nc.dram_tensor("sin_t", [SEQ, 16], BF16, kind="ExternalInput").ap()
    ident_in = nc.dram_tensor("ident_in", [128, 128], BF16, kind="ExternalInput").ap()
    triu_in = nc.dram_tensor("triu_in", [128, 128], BF16, kind="ExternalInput").ap()
    ones1_in = nc.dram_tensor("ones1_in", [1, 128], F32, kind="ExternalInput").ap()
    bdiag_in = nc.dram_tensor("bdiag_in", [128, 128], BF16, kind="ExternalInput").ap()

    y = nc.dram_tensor("y", [SEQ, D], F32, kind="ExternalOutput").ap()
    xr1 = nc.dram_tensor("xr1", [SEQ, D], F32, kind="Internal").ap()
    xr2 = nc.dram_tensor("xr2", [SEQ, D], F32, kind="Internal").ap()
    shear_scr = nc.dram_tensor("shear_scr", [8, 128, 512], F32, kind="Internal")
    router_scr = nc.dram_tensor("router_scr", [8, SEQ], F32, kind="Internal")
    thr_scr = nc.dram_tensor("thr_scr", [128], F32, kind="Internal")

    with contextlib.ExitStack() as stack, nc.Block() as block:
        c = Kc(nc, stack, block)
        c.xhatT = c.sb("xhatT", [128, HCH * SEQ], BF16)
        c.big2 = c.sb("big2", [128, NT * 1024], BF16)
        c.ident = c.sb("ident", [128, 128], BF16)
        c.triu = c.sb("triu", [128, 128], BF16)
        c.ones1 = c.sb("ones1", [1, 128], F32)
        c.bdiag = c.sb("bdiag", [128, 128], BF16)
        c.pSa = c.ps("pSa", [128, 1024], F32)
        c.pSb = c.ps("pSb", [128, 1024], F32)
        c.pACC = c.ps("pACC", [128, 512], F32)
        c.pW = c.ps("pW", [128, 512], F32)
        c.pC = c.ps("pC", [128, 512], F32)
        c.pT = c.ps("pT", [128, 1024], BF16)
        c.pA = c.pSa[:, 0:512]
        c.pB = c.pSb[:, 0:512]
        c.pD = c.pSa[:, 512:1024]
        c.pTa = c.pT
        c.pTb = c.pT

        @c.block.sync
        def _(sync):
            c.inc(sync.dma_start(out=c.ident[:], in_=ident_in), "const", 16)
            c.inc(sync.dma_start(out=c.triu[:], in_=triu_in), "const", 16)
            c.inc(sync.dma_start(out=c.ones1[:], in_=ones1_in), "const", 16)
            c.inc(sync.dma_start(out=c.bdiag[:], in_=bdiag_in), "const", 16)

        @c.block.tensor
        def _(tensor):
            c.wait(tensor, "const")

        @c.block.vector
        def _(vector):
            c.wait(vector, "const")

        dbg_xhatT = c.dbg("xhatT", [128, HCH * SEQ], BF16)

        emit_ln(c, 0, x_in, c.xhatT, (c.pTa, c.pTb))

        if dbg_xhatT is not None:
            @c.block.sync
            def _(sync):
                c.wait(sync, "s0lnev")
                c.dbg_store(sync, "xhatT", c.xhatT[:])

        if stage >= 2:
            emit_ffn(c, 0, x_in, xr1 if stage >= 5 else y,
                     wq, wo_d, wo_m, r_w, cos_t, sin_t,
                     router_scr, thr_scr)
        if stage >= 5:
            emit_ln(c, 1, xr1, c.xhatT, (c.pTa, c.pTb), after=("s0ep", c.val("s0ep")))
            emit_attn(c, 1, xr1, xr2 if stage >= 6 else y,
                      w_inT, b_in, b_in_r, outwT, out_b, relpeT, shear_scr)
        if stage >= 6:
            emit_ln(c, 2, xr2, c.xhatT, (c.pTa, c.pTb), after=("s1ep", c.val("s1ep")))
            emit_ffn(c, 2, xr2, y, wq, wo_d, wo_m, r_w, cos_t, sin_t,
                     router_scr, thr_scr)

        if stage < 2:
            @c.block.sync
            def _(sync):
                c.inc(sync.dma_start(out=y, in_=x_in), "yst", 16)

        @c.block.sync
        def _(sync):
            for name in ("yst", "dbgst", "s0ep", "s1ep", "s2ep"):
                if name in c._sems:
                    c.wait(sync, name)
    return nc


def _host_inputs(inputs):
    f32c = lambda a: np.ascontiguousarray(np.asarray(a), dtype=np.float32)
    wiT = f32c(np.asarray(inputs["L0_attn_in_proj_weight"]).T)
    b_in_r = f32c(np.asarray(inputs["L0_attn_in_proj_bias"]).reshape(24, 128))
    b_in = f32c(b_in_r.T)
    outwT = f32c(np.asarray(inputs["L0_attn_out_w"]).T)
    out_b = f32c(np.asarray(inputs["L0_attn_out_b"]).reshape(1, D))
    relpeT = f32c(np.asarray(inputs["L0_attn_rel_pe"]).T)
    r_w = f32c(inputs["L0_ffn_mosa_router_w"])
    wqkv_d = np.asarray(inputs["L0_ffn_dense_wqkv"])
    wqkv_m = np.asarray(inputs["L0_ffn_mosa_wqkv"])
    w = {}
    for br, wq in (("d", wqkv_d), ("m", wqkv_m)):
        for i, m in enumerate(("q", "k", "v")):
            w[f"w_{br}{m}"] = f32c(wq[:, :, 64 * i:64 * (i + 1)].reshape(D, 512))
    wo_d = f32c(inputs["L0_ffn_dense_wo"])
    wo_m = f32c(inputs["L0_ffn_mosa_wo"])
    half = 16
    inv = (10000.0 ** (-np.arange(half, dtype=np.float64) / half))
    ang = np.arange(SEQ, dtype=np.float64)[:, None] * inv[None, :]
    cos_t = np.cos(ang).astype(ml_dtypes.bfloat16)
    sin_t = np.sin(ang).astype(ml_dtypes.bfloat16)
    ident = np.eye(128).astype(ml_dtypes.bfloat16)
    triu = np.triu(np.ones((128, 128))).astype(ml_dtypes.bfloat16)
    ones1 = np.ones((1, 128), np.float32)
    bdiag = np.kron(np.eye(8), np.ones((16, 16))).astype(ml_dtypes.bfloat16)

    x = np.asarray(inputs["L0_attn_q"], dtype=np.float32)
    shared = dict(w_inT=wiT, b_in=b_in, b_in_r=b_in_r, outwT=outwT, out_b=out_b, relpeT=relpeT,
                  r_w=r_w, wo_d=wo_d, wo_m=wo_m, cos_t=cos_t, sin_t=sin_t,
                  ident_in=ident, triu_in=triu, ones1_in=ones1, bdiag_in=bdiag, **w)
    in_maps = []
    for b in range(8):
        m = dict(shared)
        m["x_in"] = np.ascontiguousarray(x[b])
        in_maps.append(m)
    return in_maps


_CACHED = {}


def kernel(**inputs):
    in_maps = _host_inputs(inputs)
    key = ("prog", BUILD_STAGE, tuple(sorted(DBG)))
    if key not in _CACHED:
        _CACHED[key] = build_program(BUILD_STAGE)
    nc = _CACHED[key]
    res = run_bass_kernel_spmd(nc, in_maps, core_ids=list(range(8)))
    out = np.stack([np.asarray(r["y"]) for r in res.results])
    kernel.last_results = res
    return out


def emit_ffn(c, si, x_src, x_dst, wq_dram, wo_d_dram, wo_m_dram, r_w_dram,
             cos_dram, sin_dram, router_scr, thr_scr):
    nc = c.nc
    p = f"s{si}f"
    lnev = (f"s{si}ln" + "ev", c.val(f"s{si}ln" + "ev"))

    r_w_sb = c.shared("f_rw", [128, 64], BF16)
    cos_sb = c.shared("f_cos", [128, NT * 16], BF16)
    sin_sb = c.shared("f_sin", [128, NT * 16], BF16)
    _xta = c.shared("ln_xta", [128, 2 * D])
    scb = _xta[0:8, 1024:1536]
    snc = _xta[:, 0:256]
    cmp_t = _xta[:, 256:512]
    sig_t = _xta[:, 512:768]
    mask_t = _xta[:, 768:1024]
    bis = c.shared("f_bis", [128, 8])
    cpart = c.shared("f_cpart", [128, 1])
    cpartb = c.shared("f_cpartb", [128, 1], BF16)
    thr8 = c.shared("f_thr8", [1, 8])
    thr_rep = c.shared("f_threp", [128, 8])
    mask_all = c.shared("f_biasall", [128, 256])
    gate_all = c.shared("f_gateall", [128, 256])
    wbuf = c.shared("wbuf", [128, 8192], BF16)
    wpair = wbuf
    q_sb = c.shared("f_qsb", [128, NT * 128], BF16)
    k_sb = c.shared("f_ksb", [128, NT * 128], BF16)
    v_aug = c.shared("f_vaug", [128, NT * 130], BF16)
    xta_f32 = c.shared("ln_xta", [128, 2 * D])
    xtb_f32 = c.shared("ln_xtb", [128, 2 * D])
    rt_all = xta_f32[:].bitcast(BF16)
    rt = [rt_all[:, 1024 * i:1024 * (i + 1)] for i in range(4)]
    qT = c.shared("f_qT", [128, SEQ], BF16)
    kT = q_sb
    PT = c.shared("f_PT", [128, 2 * 512], BF16)
    recip4 = c.shared("f_recip", [128, 4])
    scale4 = c.shared("f_scale", [128, 4])
    wo_sb = wbuf
    ep_tmp = c.shared("ln_xhat", [128, 2 * D], BF16)[:].bitcast(F32)
    ep_x = xta_f32
    ep_y = xtb_f32
    ones1 = c.ones1
    bdiag = c.bdiag

    bout = c.big2

    @c.block.gpsimd
    def _(gpsimd):
        c.wait_role(gpsimd, "awm")
        c.inc(gpsimd.dma_start(
            out=r_w_sb[:].rearrange("p (h n) -> p h n", h=HCH),
            in_=r_w_dram.rearrange("(h p) n -> p h n", p=128)), p + "cwld", 16)
        c.inc(gpsimd.dma_start(
            out=cos_sb[:].rearrange("p (t k) -> p t k", t=NT),
            in_=cos_dram.rearrange("(t p) k -> p t k", p=128)), p + "cwld", 16)
        c.inc(gpsimd.dma_start(
            out=sin_sb[:].rearrange("p (t k) -> p t k", t=NT),
            in_=sin_dram.rearrange("(t p) k -> p t k", p=128)), p + "cwld", 16)
        _ffn_pair_weights(c, p, gpsimd, wq_dram, wpair, 0, 0)

    @c.block.vector
    def _(vector):
        va = AP(v_aug, 64, [[NT * 130, 128], [130, NT], [65, 2]])
        c.inc(nc.vector.memset(va, 1.0), p + "vones", 1)

    @c.block.tensor
    def _(tensor):
        c.wait(tensor, f"s{si}ln" + "ev")
        c.wait(tensor, p + "cwld")
        for f in range(HCH):
            for h in range(HCH):
                ins = nc.tensor.matmul(c.pC[0:8, :], r_w_sb[:, 8 * h:8 * h + 8],
                                       c.xhatT[:, SEQ * h + 512 * f:SEQ * h + 512 * (f + 1)],
                                       start=(h == 0), stop=(h == HCH - 1))
            c.inc(ins, p + "scm", 1)
            if f < HCH - 1:
                c.wait(tensor, p + "scev", f + 1)
        for t in range(NT):
            for h in range(HCH):
                ins = nc.tensor.matmul(c.pD[:, 8 * t:8 * t + 8],
                                       c.xhatT[:, SEQ * h + 128 * t:SEQ * h + 128 * (t + 1)],
                                       r_w_sb[:, 8 * h:8 * h + 8],
                                       start=(t == 0 and h == 0),
                                       stop=(t == NT - 1 and h == HCH - 1))
        c.inc(ins, p + "stm", 1)

    @c.block.vector
    def _(vector):
        for f in range(HCH):
            c.wait(vector, p + "scm", c.val(p + "scm") - HCH + f + 1)
            if f >= 1:
                c.wait(vector, p + "scwr", 16 * f)
            c.inc(nc.vector.tensor_copy(scb, c.pC[0:8, :]), p + "scev", 1)

    @c.block.sync
    def _(sync):
        for f in range(HCH):
            c.wait(sync, p + "scev", c.val(p + "scev") - HCH + f + 1)
            c.inc(sync.dma_start(out=router_scr.ap()[:, 512 * f:512 * (f + 1)], in_=scb),
                  p + "scwr", 16)
        c.wait(sync, p + "scwr")
        c.inc(sync.dma_start(out=snc, in_=router_scr.ap().rearrange("n (c w) -> (n c) w", c=16)),
              p + "scrd", 16)

    LO, HI, THR, STEP, G, GS, T2, CNT = range(8)

    @c.block.vector
    def _(vector):
        c.wait(vector, p + "scrd")
        nc.vector.memset(bis[:, LO:LO + 1], -24.0)
        nc.vector.memset(bis[:, HI:HI + 1], 24.0)
        nc.vector.drain()

    for it in range(30):
        @c.block.vector
        def _(vector):
            nc.vector.tensor_tensor(bis[:, STEP:STEP + 1], bis[:, HI:HI + 1],
                                    bis[:, LO:LO + 1], ALU.subtract)
            nc.vector.drain()
            nc.vector.tensor_scalar_mul(bis[:, STEP:STEP + 1], bis[:, STEP:STEP + 1], 0.5)
            nc.vector.drain()
            nc.vector.tensor_tensor(bis[:, THR:THR + 1], bis[:, LO:LO + 1],
                                    bis[:, STEP:STEP + 1], ALU.add)
            nc.vector.drain()
            nc.vector.tensor_scalar(cmp_t, snc, bis[:, THR:THR + 1], None, ALU.is_gt)
            nc.vector.drain()
            nc.vector.reduce_sum(cpart[:], cmp_t, axis=AX.X)
            nc.vector.drain()
            c.inc(nc.vector.tensor_copy(cpartb[:], cpart[:]), p + "bv", 1)

        @c.block.tensor
        def _(tensor):
            c.wait(tensor, p + "bv")
            if it >= 1:
                c.wait(tensor, p + "bu", it)
            c.inc(nc.tensor.matmul(c.pC[:, 508 + (it % 4):509 + (it % 4)], bdiag[:], cpartb[:],
                                   start=True, stop=True), p + "bt", 1)

        @c.block.vector
        def _(vector):
            c.wait(vector, p + "bt")
            nc.vector.tensor_copy(bis[:, CNT:CNT + 1], c.pC[:, 508 + (it % 4):509 + (it % 4)])
            nc.vector.drain()
            c.inc(nc.vector.nop(), p + "bu", 1)
            nc.vector.tensor_scalar(bis[:, G:G + 1], bis[:, CNT:CNT + 1], 512.0, None, ALU.is_gt)
            nc.vector.drain()
            nc.vector.tensor_tensor(bis[:, GS:GS + 1], bis[:, G:G + 1],
                                    bis[:, STEP:STEP + 1], ALU.mult)
            nc.vector.drain()
            nc.vector.tensor_tensor(bis[:, LO:LO + 1], bis[:, LO:LO + 1],
                                    bis[:, GS:GS + 1], ALU.add)
            nc.vector.tensor_tensor(bis[:, T2:T2 + 1], bis[:, STEP:STEP + 1],
                                    bis[:, GS:GS + 1], ALU.subtract)
            nc.vector.drain()
            nc.vector.tensor_tensor(bis[:, HI:HI + 1], bis[:, HI:HI + 1],
                                    bis[:, T2:T2 + 1], ALU.subtract)
            c.inc(nc.vector.drain(), p + "bi", 1)

    @c.block.sync
    def _(sync):
        c.wait(sync, p + "bi")
        c.inc(sync.dma_start(out=thr_scr.ap(), in_=bis[:, HI:HI + 1]), p + "thw", 16)
        c.wait(sync, p + "thw")
        c.inc(sync.dma_start(out=thr8[:], in_=AP(thr_scr, 0, [[1, 1], [16, 8]]),
                             allow_slow_non_contiguous=True), p + "thr", 16)

    @c.block.tensor
    def _(tensor):
        c.wait(tensor, p + "thr")
        c.inc(nc.tensor.matmul(c.pC[:, 496:504], ones1[:], thr8[:], start=True, stop=True),
              p + "thm", 1)

    @c.block.scalar
    def _(scalar):
        c.wait(scalar, p + "stm")
        c.inc(nc.scalar.activation(sig_t, c.pD[:, 0:256], AF.Sigmoid), p + "sg", 1)

    @c.block.vector
    def _(vector):
        c.wait(vector, p + "thm")
        nc.vector.tensor_copy(thr_rep[:], c.pC[:, 496:504])
        nc.vector.drain()
        c.wait(vector, p + "sg")
        nc.vector.tensor_tensor(mask_all[:].rearrange("q (t n) -> q t n", n=8),
                                c.pD[:, 0:256].rearrange("q (t n) -> q t n", n=8),
                                thr_rep[:].unsqueeze(1).broadcast_to([128, NT, 8]),
                                ALU.is_gt)
        nc.vector.drain()
        nc.vector.tensor_tensor(gate_all[:], mask_all[:], sig_t, ALU.mult)
        c.inc(nc.vector.drain(), p + "msk", 1)

    dbg_rt = c.dbg(f"s{si}router", [128, 3 * 256])
    if dbg_rt is not None:
        @c.block.sync
        def _(sync):
            c.wait(sync, p + "msk")
            c.inc(sync.dma_start(out=dbg_rt[:, 0:256], in_=mask_all[:]), "dbgst", 16)
            c.inc(sync.dma_start(out=dbg_rt[:, 256:512], in_=gate_all[:]), "dbgst", 16)
            c.inc(sync.dma_start(out=dbg_rt[:, 512:520], in_=thr_rep[:]), "dbgst", 16)

    for itn in range(8):
        br, pr = itn // 4, itn % 4
        _ffn_pair(c, si, p, itn, br, pr, wq_dram, wpair, q_sb, k_sb, v_aug, rt,
                  qT, kT, PT, recip4, scale4, mask_all, gate_all, bout,
                  cos_sb, sin_sb)

    _ffn_wo_epilogue(c, si, p, x_src, x_dst, bout, wo_sb, ep_tmp, ep_x, ep_y,
                     wo_d_dram, wo_m_dram)


def _ffn_pair_weights(c, p, gpsimd, wq_dram, wpair, itn, buf):
    br, pr = itn // 4, itn % 4
    names = ("w_dq", "w_dk", "w_dv") if br == 0 else ("w_mq", "w_mk", "w_mv")
    for mi, nm in enumerate(names):
        src = wq_dram[nm][:, 128 * pr:128 * (pr + 1)]
        dst = AP(wpair, 3072 * buf + 128 * mi, [[8192, 128], [384, HCH], [1, 128]])
        c.inc(gpsimd.dma_start(out=dst, in_=src.rearrange("(h p) j -> p h j", p=128)),
              p + "wld" + str(itn % 2), 16)


def _ffn_pair(c, si, p, itn, br, pr, wq_dram, wpair, q_sb, k_sb, v_aug, rt,
              qT, kT, PT, recip4, scale4, mask_all, gate_all, bout, cos_sb, sin_sb):
    nc = c.nc
    buf = itn % 2
    wld_needed = c.val(p + "wld" + str(itn % 2))

    if itn + 1 < 8:
        @c.block.gpsimd
        def _(gpsimd):
            if itn >= 1:
                c.wait(gpsimd, p + "qm", 32 * itn)
            _ffn_pair_weights(c, p, gpsimd, wq_dram, wpair, itn + 1, (itn + 1) % 2)

    qm0 = c.val(p + "qm")
    qe0 = c.val(p + "qe")

    @c.block.tensor
    def _(tensor):
        c.wait(tensor, p + "wld" + str(itn % 2), wld_needed)
        if itn == 0:
            c.wait(tensor, p + "msk")
            c.wait(tensor, p + "vones")
        for t in range(NT):
            ps = (c.pW, c.pC)[t % 2]
            if t >= 2:
                c.wait(tensor, p + "qe", qe0 + t - 1)
            for h in range(HCH):
                ins = nc.tensor.matmul(
                    ps[:, 0:384],
                    c.xhatT[:, SEQ * h + 128 * t:SEQ * h + 128 * (t + 1)],
                    wpair[:, 3072 * buf + 384 * h:3072 * buf + 384 * (h + 1)],
                    start=(h == 0), stop=(h == HCH - 1))
            c.inc(ins, p + "qm", 1)

    @c.block.vector
    def _(vector):
        if itn >= 1:
            c.wait(vector, p + "avk0", c.val(p + "avk0"))
            c.wait(vector, p + "avk1", c.val(p + "avk1"))
        for t in range(NT):
            c.wait(vector, p + "qm", qm0 + t + 1)
            ps = (c.pW, c.pC)[t % 2]
            nc.vector.tensor_copy(q_sb[:, 128 * t:128 * (t + 1)], ps[:, 0:128])
            nc.vector.tensor_copy(k_sb[:, 128 * t:128 * (t + 1)], ps[:, 128:256])
            c.inc(nc.vector.tensor_copy(
                AP(v_aug, 130 * t, [[NT * 130, 128], [65, 2], [1, 64]]),
                ps[:, 256:384].rearrange("q (g d) -> q g d", g=2)), p + "qe", 1)

    @c.block.vector
    def _(vector):
        c.wait(vector, p + "qe", qe0 + NT)
        for mat in (q_sb, k_sb):
            x1 = AP(mat, 0, [[NT * 128, 128], [128, NT], [64, 2], [1, 16]])
            x2 = AP(mat, 16, [[NT * 128, 128], [128, NT], [64, 2], [1, 16]])
            cosv = AP(cos_sb, 0, [[NT * 16, 128], [16, NT], [0, 2], [1, 16]])
            sinv = AP(sin_sb, 0, [[NT * 16, 128], [16, NT], [0, 2], [1, 16]])
            sh = [128, NT, 2, 16]
            r0 = rt[0][:, 0:1024].rearrange("p (t g d) -> p t g d", t=NT, g=2)
            r1 = rt[1][:, 0:1024].rearrange("p (t g d) -> p t g d", t=NT, g=2)
            r2 = rt[2][:, 0:1024].rearrange("p (t g d) -> p t g d", t=NT, g=2)
            r3 = rt[3][:, 0:1024].rearrange("p (t g d) -> p t g d", t=NT, g=2)
            nc.vector.tensor_tensor(r0, x1, cosv, ALU.mult)
            nc.vector.tensor_tensor(r1, x2, sinv, ALU.mult)
            nc.vector.tensor_tensor(r2, x1, sinv, ALU.mult)
            nc.vector.tensor_tensor(r3, x2, cosv, ALU.mult)
            nc.vector.drain()
            nc.vector.tensor_tensor(x1, r0, r1, ALU.subtract)
            nc.vector.tensor_tensor(x2, r2, r3, ALU.add)
            nc.vector.drain()
        c.inc(nc.vector.nop(), p + "rp", 1)

    @c.block.vector
    def _(vector):
        va = AP(v_aug, 64, [[NT * 130, 128], [130, NT], [65, 2]])
        nc.vector.memset(va, 1.0)
        nc.vector.drain()
        if br == 1:
            for h in range(2):
                hd = 2 * pr + h
                dv = AP(v_aug, 65 * h, [[NT * 130, 128], [130, NT], [1, 65]])
                nc.vector.tensor_tensor(
                    dv, dv, AP(mask_all, hd, [[256, 128], [8, NT], [0, 65]]),
                    ALU.mult)
                nc.vector.drain()
        c.inc(nc.vector.nop(), p + "vm", 1)

    rp_need = c.val(p + "rp")
    tq0 = c.val(p + "tq")
    te0 = c.val(p + "te")

    @c.block.tensor
    def _(tensor):
        c.wait(tensor, p + "rp", rp_need)
        g = 0
        for mi, mat in enumerate((q_sb, k_sb)):
            for t in range(NT):
                if t % 8 == 0:
                    if g >= 1:
                        c.wait(tensor, p + "te", te0 + g)
                    pT = (c.pTa, c.pTb)[g % 2]
                ins = nc.tensor.transpose(pT[:, 128 * (t % 8):128 * (t % 8 + 1)],
                                          mat[:, 128 * t:128 * (t + 1)], c.ident[:])
                if t % 8 == 7:
                    c.inc(ins, p + "tq", 1)
                    g += 1

    @c.block.vector
    def _(vector):
        for g in range(8):
            mi, tg = g // 4, g % 4
            c.wait(vector, p + "tq", tq0 + g + 1)
            dstT = (qT, kT)[mi]
            c.inc(nc.vector.tensor_copy(
                AP(dstT, 1024 * tg, [[SEQ, 128], [1, 1024]]),
                (c.pTa, c.pTb)[g % 2][:]), p + "te", 1)

    te_need = c.val(p + "te")

    _ffn_head(c, si, p, itn, br, pr, te_need, qT, kT, v_aug, PT,
              recip4, scale4, gate_all, bout)


def _ffn_head(c, si, p, itn, br, pr, te_need, qT, kT, v_aug, PT, recip4,
              scale4, gate_all, bout):
    nc = c.nc
    wbuf = c.shared("wbuf", [128, 8192], BF16)

    def psS(h, g):
        ps = (c.pSa, c.pSb)[h]
        return ps[:, 512 * (g % 2):512 * (g % 2) + 512]

    def pts(h, g):
        buf = (PT[:, 0:1024], wbuf[:, 6144:7168])[h]
        return buf[:, 512 * (g % 2):512 * (g % 2) + 512]

    NQB = 16
    blocks = []
    for Qb in range(NQB):
        sups = [[(2 * i, 256, 0), (2 * i + 1, 256, 0)] for i in range(Qb)]
        sups.append([(2 * Qb, 256, 0), (2 * Qb + 1, 128, 128)])
        blocks.append((Qb, sups))
    flat = []
    g = 0
    for (Qb, sups) in blocks:
        for i, sup in enumerate(sups):
            flat.append((g, Qb, sup, i == 0, i == len(sups) - 1))
            g += 1

    ex0 = [c.val(p + "ex0"), c.val(p + "ex1")]
    sm0 = [c.val(p + "sm0"), c.val(p + "sm1")]
    avk0 = [c.val(p + "avk0"), c.val(p + "avk1")]
    dm0 = c.val(p + "dm")
    nrm_before = c.val(p + "nrm")
    vm_need = c.val(p + "vm")
    ACO = (0, 260)
    dmap = {}
    nd = dm0
    for (g, Qb, sup, fb, lb) in flat:
        if lb:
            dmap[(g, 0)] = nd + 1
            dmap[(g, 1)] = nd + 2
            nd += 2

    @c.block.tensor
    def _(tensor):
        c.wait(tensor, p + "te", te_need)
        c.wait(tensor, p + "vm", vm_need)

        def emit_S(h, g, sup, Qb):
            if ex0[h] + g - 1 >= 1:
                c.wait(tensor, p + "ex" + str(h), ex0[h] + g - 1)
            ps = psS(h, g)
            for half, (kt, ncols, coff) in enumerate(sup):
                c.inc(nc.tensor.matmul(
                    ps[:, 256 * half:256 * half + ncols],
                    kT[64 * h:64 * (h + 1), 128 * kt:128 * (kt + 1)],
                    qT[64 * h:64 * (h + 1), 256 * Qb + coff:256 * (Qb + 1)],
                    start=True, stop=True), p + "sm" + str(h), 1)

        started = {}

        def emit_av(h, g, sup, Qb, fb, lb):
            if lb:
                c.wait(tensor, p + "dm", dmap[(g, h)])
            else:
                c.wait(tensor, p + "ex" + str(h), ex0[h] + g + 1)
            if fb and h == 0:
                c.wait(tensor, p + "nrm", nrm_before + Qb)
            pt = pts(h, g)
            for half, (kt, ncols, coff) in enumerate(sup):
                for j in range(coff // 128, 2):
                    ins = nc.tensor.matmul(
                        c.pACC[:, ACO[h] + 65 * j:ACO[h] + 65 * j + 65],
                        pt[:, 256 * half + 128 * j - coff:
                           256 * half + 128 * j - coff + 128],
                        v_aug[:, 130 * kt + 65 * h:130 * kt + 65 * h + 65],
                        start=(Qb not in started),
                        stop=(lb and h == 1 and half == len(sup) - 1 and j == 1))
                    started[Qb] = True
            c.inc(ins, p + "avk" + str(h), 1)

        prev = None
        for (g, Qb, sup, fb, lb) in flat:
            emit_S(0, g, sup, Qb)
            emit_S(1, g, sup, Qb)
            if prev is not None:
                emit_av(0, *prev)
                emit_av(1, *prev)
            prev = (g, sup, Qb, fb, lb)
        emit_av(0, *prev)
        emit_av(1, *prev)

    @c.block.scalar
    def _(scalar):
        for (g, Qb, sup, fb, lb) in flat:
            fd = 256 * (len(sup) - 1) + sup[-1][1]
            for h in range(2):
                c.wait(scalar, p + "sm" + str(h), sm0[h] + 2 * (g + 1))
                if avk0[h] + g - 1 >= 1:
                    c.wait(scalar, p + "avk" + str(h), avk0[h] + g - 1)
                c.inc(nc.scalar.activation(pts(h, g)[:, 0:fd], psS(h, g)[:, 0:fd],
                                           AF.Exp, scale=0.125), p + "ex" + str(h), 1)

    @c.block.vector
    def _(vector):
        for (g, Qb, sup, fb, lb) in flat:
            if lb:
                for h in range(2):
                    c.wait(vector, p + "ex" + str(h), ex0[h] + g + 1)
                    pt = pts(h, g)
                    nc.vector.tensor_tensor(pt[:, 0:128], pt[:, 0:128],
                                            c.triu[:], ALU.mult)
                    nc.vector.tensor_tensor(pt[:, 256:384], pt[:, 256:384],
                                            c.triu[:], ALU.mult)
                    c.inc(nc.vector.drain(), p + "dm", 1)
                c.wait(vector, p + "avk0", avk0[0] + g + 1)
                c.wait(vector, p + "avk1", avk0[1] + g + 1)
                for h in range(2):
                    hd = 2 * pr + h
                    nc.vector.tensor_scalar(recip4[:, 0:2],
                                            AP(c.pACC, ACO[h] + 64, [[512, 128], [65, 2]]),
                                            1e-30, None, ALU.add)
                    nc.vector.drain()
                    nc.vector.reciprocal(recip4[:, 0:2], recip4[:, 0:2])
                    nc.vector.drain()
                    if br == 1:
                        nc.vector.tensor_tensor(scale4[:, 0:2], recip4[:, 0:2],
                                                AP(gate_all, 8 * 2 * Qb + hd, [[256, 128], [8, 2]]),
                                                ALU.mult)
                        nc.vector.drain()
                        sc = scale4
                    else:
                        sc = recip4
                    dst = AP(bout, br * (NT * 512) + 512 * 2 * Qb + 64 * hd,
                             [[NT * 1024, 128], [512, 2], [1, 64]])
                    src = AP(c.pACC, ACO[h], [[512, 128], [65, 2], [1, 64]])
                    nc.vector.tensor_tensor(dst, src,
                                            sc[:, 0:2].unsqueeze(2).broadcast_to([128, 2, 64]),
                                            ALU.mult)
                    nc.vector.drain()
                c.inc(nc.vector.nop(), p + "nrm", 1)


def _ffn_wo_epilogue(c, si, p, x_src, x_dst, bout, wo_sb, ep_tmp, ep_x, ep_y,
                     wo_d_dram, wo_m_dram):
    nc = c.nc
    boutT = c.xhatT
    qm_total = c.val(p + "qm")

    @c.block.gpsimd
    def _(gpsimd):
        c.wait(gpsimd, p + "qm", qm_total)
        for h in range(2):
            c.wait(gpsimd, p + "ex" + str(h), c.val(p + "ex" + str(h)))
            c.wait(gpsimd, p + "avk" + str(h), c.val(p + "avk" + str(h)))
        for br in range(2):
            c.inc(gpsimd.dma_start(
                out=wo_sb[:, br * 4096:(br + 1) * 4096].rearrange("p (cc j) -> p cc j", cc=4),
                in_=(wo_d_dram, wo_m_dram)[br].rearrange("(cc p) j -> p cc j", p=128)),
                p + "wol", 16)
    nrm_total = c.val(p + "nrm")
    bt0 = c.val(p + "bt2")
    be0 = c.val(p + "be")
    wm0 = c.val(p + "wm")
    eps0 = c.val(p + "eps")
    epl0 = [c.val(p + "epl0"), c.val(p + "epl1")]

    dbg_bout = c.dbg(f"s{si}bout", [128, NT * 1024], BF16)
    if dbg_bout is not None:
        @c.block.sync
        def _(sync):
            c.wait(sync, p + "nrm", nrm_total)
            c.inc(sync.dma_start(out=dbg_bout, in_=bout[:]), "dbgst", 16)

    @c.block.tensor
    def _(tensor):
        c.wait(tensor, p + "nrm", nrm_total)
        for t in range(NT):
            if t >= 1:
                c.wait(tensor, p + "be", be0 + t)
            pT = (c.pTa, c.pTb)[t % 2]
            for ch in range(8):
                brx, cc = ch // 4, ch % 4
                ins = nc.tensor.transpose(
                    pT[:, 128 * ch:128 * (ch + 1)],
                    bout[:, brx * (NT * 512) + 512 * t + 128 * cc:
                         brx * (NT * 512) + 512 * t + 128 * (cc + 1)],
                    c.ident[:])
            c.inc(ins, p + "bt2", 1)

    @c.block.vector
    def _(vector):
        c.wait(vector, p + "qm", qm_total)
        for t in range(NT):
            c.wait(vector, p + "bt2", bt0 + t + 1)
            dst = AP(boutT, 128 * t, [[HCH * SEQ, 128], [SEQ, 8], [1, 128]])
            c.inc(nc.vector.tensor_copy(dst, (c.pTa, c.pTb)[t % 2][:].rearrange(
                "q (ch t) -> q ch t", ch=8)), p + "be", 1)

    @c.block.tensor
    def _(tensor):
        c.wait(tensor, p + "wol", 32)
        for t in range(NT):
            c.wait(tensor, p + "be", be0 + t + 1)
            if t >= 2:
                c.wait(tensor, p + "eps", eps0 + t - 1)
            for e in range(2):
                ps = ((c.pSa[:, 0:512], c.pSa[:, 512:1024]) if t % 2 == 0
                      else (c.pSb[:, 0:512], c.pSb[:, 512:1024]))[e]
                first = True
                for brx in range(2):
                    for cc in range(4):
                        ins = nc.tensor.matmul(
                            ps[:],
                            boutT[:, (4 * brx + cc) * SEQ + 128 * t:
                                  (4 * brx + cc) * SEQ + 128 * (t + 1)],
                            wo_sb[:, 4096 * brx + 1024 * cc + 512 * e:
                                  4096 * brx + 1024 * cc + 512 * (e + 1)],
                            start=first, stop=(brx == 1 and cc == 3))
                        first = False
            c.inc(ins, p + "wm", 1)

    @c.block.sync
    def _(sync):
        c.wait(sync, p + "rp")
        for t in range(NT + 2):
            if t >= 2:
                c.wait(sync, p + "epv", t - 1)
            if t < NT:
                c.inc(sync.dma_start(out=ep_x[:, 1024 * (t % 2):1024 * (t % 2 + 1)],
                                     in_=x_src[128 * t:128 * (t + 1), :]),
                      p + "epl" + str(t % 2), 16)
            if t >= 2:
                ts = t - 2
                c.inc(sync.dma_start(out=x_dst[128 * ts:128 * (ts + 1), :],
                                     in_=ep_y[:, 1024 * (ts % 2):1024 * (ts % 2 + 1)]),
                      p + "epw" + str(ts % 2), 16)
        c.wait(sync, p + "epw0")
        c.wait(sync, p + "epw1")
        c.inc(sync.nop(), f"s{si}ep", 1)

    @c.block.vector
    def _(vector):
        for t in range(NT):
            c.wait(vector, p + "wm", wm0 + t + 1)
            psf = (c.pSa, c.pSb)[t % 2]
            nc.vector.drain()
            c.inc(nc.vector.tensor_scalar_mul(ep_tmp[:], psf[:], 0.5), p + "eps", 1)
            c.wait(vector, p + "epl" + str(t % 2), epl0[t % 2] + 16 * (t // 2 + 1))
            if t >= 2:
                c.wait(vector, p + "epw" + str(t % 2), 16 * ((t - 2) // 2 + 1))
            nc.vector.drain()
            c.inc(nc.vector.tensor_tensor(ep_y[:, 1024 * (t % 2):1024 * (t % 2 + 1)],
                                          ep_tmp[:], ep_x[:, 1024 * (t % 2):1024 * (t % 2 + 1)],
                                          ALU.add), p + "epv", 1)





def emit_attn(c, si, x_src, x_dst, w_inT, b_in_dram, b_in_r_dram, outwT, out_b_dram,
              relpeT_dram, shear_scr):
    nc = c.nc
    p = f"s{si}a"
    lnev_name = f"s{si}ln" + "ev"

    wbuf = c.shared("wbuf", [128, 8192], BF16)
    qT = c.shared("f_qT", [128, SEQ], BF16)
    klT = c.shared("f_qsb", [128, NT * 128], BF16)
    krT = c.shared("f_ksb", [128, NT * 128], BF16)
    v_aug = c.shared("f_vaug", [128, NT * 130], BF16)
    PT = c.shared("f_PT", [128, 2 * 512], BF16)
    recip4 = c.shared("f_recip", [128, 4])
    b_in_sb = c.sb(p + "bin", [128, 24])
    relpeT_sb = c.sb(p + "relpe", [128, 132], BF16)
    rp01 = c.sb(p + "rp01", [128, 2])
    kbias = c.sb(p + "kbias", [128, 2])
    bvrow = c.sb(p + "bvrow", [1, 128])
    bv_rep = c.sb(p + "bvrep", [128, 2 * 128])
    ep_x = c.shared("ln_xta", [128, 2 * D])
    ep_y = c.shared("ln_xtb", [128, 2 * D])
    e_sb = ep_x[:, 0:640]
    w_sb = ep_x[:, 640:1408]
    br_sb = ep_x[:, 1408:1666]

    _xtbb = ep_y[:].bitcast(BF16)
    ptw = _xtbb[:, 0:1536]
    ptT = _xtbb[:, 1536:2304]
    wexp = _xtbb[:, 2304:3072]
    obrow = ep_y[0:1, 1536:2048]
    outb_rep = v_aug[:].bitcast(F32)[:, 0:1024]
    ep_tmp = c.shared("ln_xhat", [128, 2 * D], BF16)[:].bitcast(F32)
    aout = c.big2

    @c.block.gpsimd
    def _(gpsimd):
        c.wait_role(gpsimd, "fwm")
        c.inc(gpsimd.dma_start(out=b_in_sb[:], in_=b_in_dram), p + "cw", 16)
        c.inc(gpsimd.dma_start(out=relpeT_sb[0:64, 0:129], in_=relpeT_dram), p + "cw", 16)
        c.inc(gpsimd.dma_start(out=relpeT_sb[64:128, 0:129], in_=relpeT_dram), p + "cw", 16)
        for half in range(2):
            c.inc(gpsimd.dma_start(out=rp01[64 * half:64 * (half + 1), 0:1],
                                   in_=relpeT_dram[:, 0:1],
                                   allow_slow_non_contiguous=True), p + "cw", 16)
            c.inc(gpsimd.dma_start(out=rp01[64 * half:64 * (half + 1), 1:2],
                                   in_=relpeT_dram[:, 128:129],
                                   allow_slow_non_contiguous=True), p + "cw", 16)
        _attn_pair_weights(c, p, gpsimd, w_inT, wbuf, 0)

    @c.block.vector
    def _(vector):
        c.wait(vector, p + "cw", 16 * 7)
        c.wait_role(vector, "lnst")
        nc.vector.memset(w_sb[:, 0:192], 0.0)
        nc.vector.memset(AP(v_aug, 64, [[NT * 130, 128], [130, NT], [65, 2]]), 1.0)
        nc.vector.drain()
        c.inc(nc.vector.nop(), p + "rp01", 1)

    for pr in range(8):
        _attn_pair(c, si, p, pr, w_inT, wbuf, b_in_sb, rp01, kbias, bvrow, bv_rep,
                   b_in_r_dram, qT, klT, krT, v_aug, PT, recip4, relpeT_sb,
                   br_sb, e_sb, w_sb, ptw, ptT, wexp, shear_scr, aout)

    _attn_proj_epilogue(c, si, p, x_src, x_dst, aout, wbuf, outwT, outb_rep,
                        ep_tmp, ep_x, ep_y, obrow, out_b_dram)


def _attn_pair_weights(c, p, gpsimd, w_inT, wbuf, pr):
    buf = pr % 2
    for mi in range(3):
        src = w_inT[:, 1024 * mi + 128 * pr:1024 * mi + 128 * (pr + 1)]
        dst = wbuf[:, 3072 * buf + 1024 * mi:3072 * buf + 1024 * (mi + 1)
                   ].rearrange("p (h j) -> p h j", h=HCH)
        c.inc(gpsimd.dma_start(out=dst, in_=src.rearrange("(h p) j -> p h j", p=128)),
              p + "wld" + str(pr % 2), 16)


def _attn_pair(c, si, p, pr, w_inT, wbuf, b_in_sb, rp01, kbias, bvrow, bv_rep,
               b_in_r_dram, qT, klT, krT, v_aug, PT, recip4, relpeT_sb,
               br_sb, e_sb, w_sb, ptw, ptT, wexp, shear_scr, aout):
    nc = c.nc
    buf = pr % 2
    wld_needed = c.val(p + "wld" + str(pr % 2))

    if pr + 1 < 8:
        @c.block.gpsimd
        def _(gpsimd):
            if pr >= 1:
                c.wait(gpsimd, p + "qm", 49 * pr)
            _attn_pair_weights(c, p, gpsimd, w_inT, wbuf, pr + 1)

    @c.block.vector
    def _(vector):
        if pr == 0:
            c.wait(vector, p + "rp01")
        if pr >= 1:
            c.wait(vector, p + "qev", c.val(p + "qev"))
        nc.vector.tensor_scalar(kbias[:, 0:1], rp01[:, 0:1], b_in_sb[:, 8 + pr:9 + pr],
                                None, ALU.add)
        nc.vector.tensor_scalar(kbias[:, 1:2], rp01[:, 1:2], b_in_sb[:, 8 + pr:9 + pr],
                                None, ALU.add)
        c.inc(nc.vector.drain(), p + "kb", 1)

    @c.block.sync
    def _(sync):
        if pr >= 1:
            c.wait(sync, p + "qm", 49 * (pr - 1) + 1)
        c.inc(sync.dma_start(out=bvrow[:], in_=b_in_r_dram[16 + pr:17 + pr, :]),
              p + "bvl" + str(pr % 2), 16)

    qm0 = c.val(p + "qm")
    qev0 = c.val(p + "qev")
    kb_need = c.val(p + "kb")

    @c.block.tensor
    def _(tensor):
        c.wait(tensor, p + "wld" + str(pr % 2), wld_needed)
        c.wait(tensor, p + "bvl" + str(pr % 2), c.val(p + "bvl" + str(pr % 2)))
        if pr == 0:
            c.wait(tensor, lnev_name_global(si))
        else:
            c.wait(tensor, p + "wex", c.val(p + "wex"))
            c.wait(tensor, p + "wbe", c.val(p + "wbe"))
        c.inc(nc.tensor.matmul(c.pC[:, 0:128], c.ones1[:], bvrow[:], start=True, stop=True),
              p + "qm", 1)
        for mi in range(2):
            for f in range(HCH):
                ps = (c.pW, c.pC)[f % 2]
                gi = mi * HCH + f
                if gi == 1:
                    c.wait(tensor, p + "abv", c.val(p + "abv") + 1)
                if gi >= 2:
                    c.wait(tensor, p + "qev", qev0 + gi - 1)
                for h in range(HCH):
                    ins = nc.tensor.matmul(
                        ps[:],
                        wbuf[:, 3072 * buf + 1024 * mi + 128 * h:
                             3072 * buf + 1024 * mi + 128 * (h + 1)],
                        c.xhatT[:, SEQ * h + 512 * f:SEQ * h + 512 * (f + 1)],
                        start=(h == 0), stop=(h == HCH - 1))
                c.inc(ins, p + "qm", 1)
        for t in range(NT):
            ps = (c.pW, c.pC)[t % 2]
            c.wait(tensor, p + "qev", qev0 + 15 + t)
            for h in range(HCH):
                ins = nc.tensor.matmul(
                    ps[:, 0:128],
                    c.xhatT[:, SEQ * h + 128 * t:SEQ * h + 128 * (t + 1)],
                    wbuf[:, 3072 * buf + 2048 + 128 * h:3072 * buf + 2048 + 128 * (h + 1)],
                    start=(h == 0), stop=(h == HCH - 1))
            c.inc(ins, p + "qm", 1)

    @c.block.vector
    def _(vector):
        c.wait(vector, p + "qm", qm0 + 1)
        nc.vector.tensor_copy(bv_rep[:, 128 * buf:128 * (buf + 1)], c.pC[:, 0:128])
        c.inc(nc.vector.drain(), p + "abv", 1)
        c.wait(vector, p + "kb", kb_need)
        for mi in range(2):
            for f in range(HCH):
                ps = (c.pW, c.pC)[f % 2]
                gi = mi * HCH + f
                c.wait(vector, p + "qm", qm0 + 1 + gi + 1)
                if mi == 0:
                    c.inc(nc.vector.tensor_scalar(
                        qT[:, 512 * f:512 * (f + 1)], ps[:],
                        b_in_sb[:, pr:pr + 1], 0.125, ALU.add, ALU.mult), p + "qev", 1)
                else:
                    nc.vector.tensor_scalar(klT[:, 512 * f:512 * (f + 1)], ps[:],
                                            kbias[:, 0:1], None, ALU.add)
                    c.inc(nc.vector.tensor_scalar(krT[:, 512 * f:512 * (f + 1)], ps[:],
                                                  kbias[:, 1:2], None, ALU.add),
                          p + "qev", 1)
        for t in range(NT):
            ps = (c.pW, c.pC)[t % 2]
            c.wait(vector, p + "qm", qm0 + 17 + t + 1)
            dstv = AP(v_aug, 130 * t, [[NT * 130, 128], [65, 2], [1, 64]])
            c.inc(nc.vector.tensor_tensor(
                dstv, ps[:, 0:128].rearrange("q (g d) -> q g d", g=2),
                bv_rep[:, 128 * buf:128 * (buf + 1)].rearrange("q (g d) -> q g d", g=2),
                ALU.add), p + "qev", 1)

    qev_need = c.val(p + "qev")
    _attn_head(c, si, p, pr, qev_need, qT, klT, krT, v_aug, PT, recip4,
               relpeT_sb, br_sb, e_sb, w_sb, ptw, ptT, wexp, shear_scr, aout)


def lnev_name_global(si):
    return f"s{si}ln" + "ev"


def _attn_head(c, si, p, pr, qev_need, qT, klT, krT, v_aug, PT, recip4,
               relpeT_sb, br_sb, e_sb, w_sb, ptw, ptT, wexp, shear_scr, aout):
    nc = c.nc
    wbuf = c.shared("wbuf", [128, 8192], BF16)

    def pts(h):
        return (PT[:, 0:1024], wbuf[:, 6144:7168])[h]

    NQB = 16
    blocks = []
    for Qb in range(NQB):
        subs = []
        for kt in range(0, 2 * Qb - 1):
            subs.append((kt, 0))
        for kt in range(2 * Qb + 3, NT):
            subs.append((kt, 1))
        pieces = []
        for j in range(2):
            T = 2 * Qb + j
            for kk in range(max(2 * Qb - 1, 0), T - 1):
                pieces.append((kk, 0, j))
            for kk in range(T + 2, 2 * Qb + 3):
                if kk < NT:
                    pieces.append((kk, 1, j))
        sups = []
        i = 0
        while i < len(subs):
            sups.append([subs[i + k] for k in range(min(4, len(subs) - i))])
            i += 4
        sups += [[x] for x in pieces]
        wins = []
        for j in range(2):
            qt = 128 * (2 * Qb + j)
            wins.append((j, qt, max(0, qt - 128), min(SEQ, qt + 256)))
        blocks.append((Qb, sups, wins))

    ex0 = [c.val(p + "ex0"), c.val(p + "ex1")]
    sm0 = [c.val(p + "sm0"), c.val(p + "sm1")]
    avk0 = [c.val(p + "avk0"), c.val(p + "avk1")]
    nrm_before = c.val(p + "nrm")
    wi0 = c.val(p + "wsm")
    ACO = (0, 260)

    @c.block.tensor
    def _(tensor):
        c.wait(tensor, p + "qev", qev_need)
        g2 = 0
        wi = 0
        started_blk = [False]

        def emit_S(h, g, sup, Qb):
            psS = (c.pSa, c.pSb)[h]
            c.wait(tensor, p + "ex" + str(h), ex0[h] + g)
            for half, sub in enumerate(sup):
                kt, var = sub[0], sub[1]
                kTv = (klT, krT)[var]
                if len(sub) == 2:
                    rhs = qT[64 * h:64 * (h + 1), 256 * Qb:256 * (Qb + 1)]
                    nq = 256
                else:
                    j = sub[2]
                    rhs = qT[64 * h:64 * (h + 1),
                             256 * Qb + 128 * j:256 * Qb + 128 * (j + 1)]
                    nq = 128
                c.inc(nc.tensor.matmul(psS[:, 256 * half:256 * half + nq],
                                       kTv[64 * h:64 * (h + 1), 128 * kt:128 * (kt + 1)],
                                       rhs, start=True, stop=True), p + "sm" + str(h), 1)

        def emit_av(h, g, sup, Qb):
            c.wait(tensor, p + "ex" + str(h), ex0[h] + g + 1)
            for half, sub in enumerate(sup):
                kt = sub[0]
                jr = range(2) if len(sub) == 2 else [sub[2]]
                for j in jr:
                    pcol = 128 * j if len(sub) == 2 else 0
                    ins = nc.tensor.matmul(
                        c.pACC[:, ACO[h] + 65 * j:ACO[h] + 65 * j + 65],
                        pts(h)[:, 256 * half + pcol:256 * half + pcol + 128],
                        v_aug[:, 130 * kt + 65 * h:130 * kt + 65 * h + 65],
                        start=(not started_blk[0]), stop=False)
                    started_blk[0] = True
            c.inc(ins, p + "avk" + str(h), 1)

        for (Qb, sups, wins) in blocks:
            started_blk[0] = False
            c.wait(tensor, p + "nrm", nrm_before + Qb)

            def w_br(widx):
                (j, qt, ws, we) = wins[widx % 2]
                h = widx // 2
                gwi = wbase + widx
                psw = (c.pW, c.pC)[gwi % 2]
                c.wait(tensor, p + "wex", max(gwi - 1, 0))
                c.inc(nc.tensor.matmul(psw[:, 0:129],
                                       qT[64 * h:64 * (h + 1), qt:qt + 128],
                                       relpeT_sb[64 * h:64 * (h + 1), 0:129],
                                       start=True, stop=True), p + "wbr", 1)

            def w_swin(widx):
                (j, qt, ws, we) = wins[widx % 2]
                h = widx // 2
                gwi = wbase + widx
                psw = (c.pW, c.pC)[gwi % 2]
                c.wait(tensor, p + "wbe", gwi + 1)
                c.inc(nc.tensor.matmul(psw[:, 0:we - ws],
                                       qT[64 * h:64 * (h + 1), qt:qt + 128],
                                       klT[64 * h:64 * (h + 1), ws:we],
                                       start=True, stop=True), p + "wsm", 1)

            wbase = wi0 + wi
            hooks = {0: [(w_br, 0), (w_br, 1)], 1: [(w_swin, 0)], 2: [(w_swin, 1)],
                     3: [(w_br, 2)], 4: [(w_swin, 2)], 5: [(w_br, 3)], 6: [(w_swin, 3)]}
            nsup = len(sups)
            prev = None
            for si2, sup in enumerate(sups):
                for (fn, widx) in hooks.get(si2, []):
                    fn(widx)
                emit_S(0, g2, sup, Qb)
                emit_S(1, g2, sup, Qb)
                if prev is not None:
                    emit_av(0, *prev)
                    emit_av(1, *prev)
                prev = (g2, sup, Qb)
                g2 += 1
            if prev is not None:
                emit_av(0, *prev)
                emit_av(1, *prev)
            for i in range(nsup, 7):
                for (fn, widx) in hooks.get(i, []):
                    fn(widx)
            for widx in range(4):
                (j, qt, ws, we) = wins[widx % 2]
                h = widx // 2
                gwi = wbase + widx
                W = we - ws
                c.wait(tensor, p + "wtt", gwi + 1)
                c.wait(tensor, p + "wte", gwi)
                for b in range(W // 128):
                    ins = nc.tensor.transpose(
                        c.pT[:, 128 * b:128 * (b + 1)],
                        ptw[:, 384 * (gwi % 4) + 128 * b:384 * (gwi % 4) + 128 * (b + 1)],
                        c.ident[:])
                c.inc(ins, p + "wtp", 1)
                c.wait(tensor, p + "wte", gwi + 1)
                for b in range(W // 128):
                    vt = ws // 128 + b
                    ins = nc.tensor.matmul(
                        c.pACC[:, ACO[h] + 65 * j:ACO[h] + 65 * j + 65],
                        ptT[:, 384 * (gwi % 2) + 128 * b:384 * (gwi % 2) + 128 * (b + 1)],
                        v_aug[:, 130 * vt + 65 * h:130 * vt + 65 * h + 65],
                        start=False, stop=(widx == 3 and b == W // 128 - 1))
                c.inc(ins, p + "wav", 1)
            wi += 4

    @c.block.scalar
    def _(scalar):
        g2 = 0
        wi = 0
        nsm = [sm0[0], sm0[1]]
        for (Qb, sups, wins) in blocks:
            def w_exp(widx):
                (j, qt, ws, we) = wins[widx % 2]
                gwi = wbase + widx
                W = we - ws
                slot = gwi % 2
                c.wait(scalar, p + "wsr" + str(slot), 16 * (gwi // 2 + 1))
                c.wait(scalar, p + "wtt", max(gwi - 1, 0))
                c.inc(nc.scalar.activation(wexp[:, 384 * slot:384 * slot + W],
                                           w_sb[:, 384 * slot:384 * slot + W], AF.Exp),
                      p + "wxw", 1)
                c.wait(scalar, p + "wsm", gwi + 1)
                c.wait(scalar, p + "wtp", max(gwi - 3, 0))
                psw = (c.pW, c.pC)[slot]
                c.inc(nc.scalar.activation(ptw[:, 384 * (gwi % 4):384 * (gwi % 4) + W],
                                           psw[:, 0:W], AF.Exp), p + "wex", 1)

            wbase = wi0 + wi
            shooks = {3: 0, 4: 1, 5: 2, 6: 3}
            nsup = len(sups)
            for si2, sup in enumerate(sups):
                if si2 in shooks:
                    w_exp(shooks[si2])
                for h in range(2):
                    nsm[h] += len(sup)
                    c.wait(scalar, p + "sm" + str(h), nsm[h])
                    c.wait(scalar, p + "avk" + str(h), avk0[h] + g2)
                    psS = (c.pSa, c.pSb)[h]
                    fd = 256 * (len(sup) - 1) + (256 if len(sup[-1]) == 2 else 128)
                    c.inc(nc.scalar.activation(pts(h)[:, 0:fd], psS[:, 0:fd], AF.Exp),
                          p + "ex" + str(h), 1)
                g2 += 1
            for i in range(nsup, 7):
                if i in shooks:
                    w_exp(shooks[i])
            wi += 4

    @c.block.vector
    def _(vector):
        g2 = 0
        wi = 0
        ndone = 0
        for (Qb, sups, wins) in blocks:
            for widx in range(4):
                (j, qt, ws, we) = wins[widx % 2]
                gwi = wi0 + wi
                W = we - ws
                slot = gwi % 2
                c.wait(vector, p + "wbr", gwi + 1)
                psw = (c.pW, c.pC)[slot]
                c.inc(nc.vector.tensor_copy(br_sb[:, 129 * slot:129 * (slot + 1)],
                                            psw[:, 0:129]), p + "wbe", 1)
                if gwi >= 2:
                    c.wait(vector, p + "wsw" + str(slot), 16 * (gwi // 2))
                nc.vector.drain()
                brs = br_sb[:, 129 * slot:129 * (slot + 1)]
                es = e_sb[:, 320 * slot:320 * (slot + 1)]
                nc.vector.tensor_scalar(es[:, 0:129], brs, brs[:, 0:1], None, ALU.subtract)
                nc.vector.tensor_scalar(es[:, 129:130], brs[:, 128:129], brs[:, 0:1],
                                        None, ALU.subtract)
                nc.vector.drain()
                nc.vector.tensor_copy(es[:, 130:320],
                                      es[:, 129:130].broadcast_to([128, 1, 190]).squeeze(1))
                c.inc(nc.vector.drain(), p + "web", 1)
                wi += 1
            for widx in range(4):
                gwi = wi0 + wi - 4 + widx
                (j, qt, ws, we) = wins[widx % 2]
                W = we - ws
                slot = gwi % 2
                c.wait(vector, p + "wex", gwi + 1)
                c.wait(vector, p + "wxw", gwi + 1)
                nc.vector.drain()
                c.inc(nc.vector.tensor_tensor(ptw[:, 384 * (gwi % 4):384 * (gwi % 4) + W],
                                              ptw[:, 384 * (gwi % 4):384 * (gwi % 4) + W],
                                              wexp[:, 384 * slot:384 * slot + W],
                                              ALU.mult),
                      p + "wtt", 1)
            for widx in range(4):
                gwi = wi0 + wi - 4 + widx
                (j, qt, ws, we) = wins[widx % 2]
                W = we - ws
                c.wait(vector, p + "wtp", gwi + 1)
                c.wait(vector, p + "wav", max(gwi - 1, 0))
                c.inc(nc.vector.tensor_copy(ptT[:, 384 * (gwi % 2):384 * (gwi % 2) + W],
                                            c.pT[:, 0:W]), p + "wte", 1)
            ndone += len(sups)
            c.wait(vector, p + "avk0", avk0[0] + ndone)
            c.wait(vector, p + "avk1", avk0[1] + ndone)
            c.wait(vector, p + "wav", wi0 + wi)
            for h in range(2):
                hd = 2 * pr + h
                nc.vector.tensor_scalar(recip4[:, 0:2],
                                        AP(c.pACC, ACO[h] + 64, [[512, 128], [65, 2]]),
                                        1e-30, None, ALU.add)
                nc.vector.drain()
                nc.vector.reciprocal(recip4[:, 0:2], recip4[:, 0:2])
                nc.vector.drain()
                dst = AP(aout, 1024 * 2 * Qb + 64 * hd, [[NT * 1024, 128], [1024, 2], [1, 64]])
                src = AP(c.pACC, ACO[h], [[512, 128], [65, 2], [1, 64]])
                nc.vector.tensor_tensor(dst, src,
                                        recip4[:, 0:2].unsqueeze(2).broadcast_to([128, 2, 64]),
                                        ALU.mult)
                nc.vector.drain()
            c.inc(nc.vector.nop(), p + "nrm", 1)

    @c.block.sync
    def _(sync):
        if pr == 0:
            c.wait(sync, p + "rp01")
            for slot in range(2):
                c.inc(sync.dma_start(
                    out=AP(shear_scr, 512 * 128 * slot, [[512, 128], [1, 192]]),
                    in_=w_sb[:, 0:192]), p + "wz", 16)
            c.wait(sync, p + "wz", 32 * (pr + 1) if False else c.val(p + "wz"))
        wi = 0
        for (Qb, sups, wins) in blocks:
            for widx in range(4):
                (j, qt, ws, we) = wins[widx % 2]
                gwi = wi0 + wi
                W = we - ws
                slot = gwi % 2
                c.wait(sync, p + "web", gwi + 1)
                c.inc(sync.dma_start(
                    out=AP(shear_scr, 512 * 128 * slot + 192, [[512, 128], [1, 320]]),
                    in_=e_sb[:, 320 * slot:320 * (slot + 1)]),
                      p + "wsw" + str(slot), 16)
                c.wait(sync, p + "wsw" + str(slot), 16 * (gwi // 2 + 1))
                if gwi >= 2:
                    c.wait(sync, p + "wxw", gwi - 1)
                off = 256 if qt == 0 else 128
                src = AP(shear_scr, 512 * 128 * slot + off, [[511, 128], [1, W]])
                c.inc(sync.dma_start(out=w_sb[:, 384 * slot:384 * slot + W], in_=src),
                      p + "wsr" + str(slot), 16)
                wi += 1


def _attn_proj_epilogue(c, si, p, x_src, x_dst, aout, wbuf, outwT, outb_rep,
                        ep_tmp, ep_x, ep_y, obrow, out_b_dram):
    nc = c.nc
    aoutT = c.xhatT
    qm_total = c.val(p + "qm")
    nrm_tot0 = c.val(p + "nrm")

    @c.block.sync
    def _(sync):
        c.wait(sync, p + "nrm", nrm_tot0)
        c.inc(sync.dma_start(out=obrow, in_=out_b_dram[:, 0:512]), p + "obl", 16)
        c.wait(sync, p + "obm", 1)
        c.inc(sync.dma_start(out=obrow, in_=out_b_dram[:, 512:1024]), p + "obl", 16)

    @c.block.tensor
    def _(tensor):
        c.wait(tensor, p + "obl", 16)
        c.wait(tensor, p + "nrm", nrm_tot0)
        c.inc(nc.tensor.matmul(c.pW[:, 0:512], c.ones1[:], obrow, start=True, stop=True),
              p + "obm", 1)
        c.wait(tensor, p + "obl", 32)
        c.inc(nc.tensor.matmul(c.pACC[:, 0:512], c.ones1[:], obrow, start=True, stop=True),
              p + "obm", 1)

    @c.block.vector
    def _(vector):
        c.wait(vector, p + "obm", 1)
        nc.vector.tensor_copy(outb_rep[:, 0:512], c.pW[:, 0:512])
        c.wait(vector, p + "obm", 2)
        c.inc(nc.vector.tensor_copy(outb_rep[:, 512:1024], c.pACC[:, 0:512]), p + "obr", 1)
    qev_total = c.val(p + "qev")
    nrm_total = c.val(p + "nrm")
    bt0 = c.val(p + "bt2")
    be0 = c.val(p + "be")
    wm0 = c.val(p + "wm")
    eps0 = c.val(p + "eps")
    epl0 = [c.val(p + "epl0"), c.val(p + "epl1")]

    @c.block.gpsimd
    def _(gpsimd):
        c.wait(gpsimd, p + "qm", qm_total)
        for h in range(2):
            c.wait(gpsimd, p + "ex" + str(h), c.val(p + "ex" + str(h)))
            c.wait(gpsimd, p + "avk" + str(h), c.val(p + "avk" + str(h)))
        dst = wbuf[:, 0:8192].rearrange("p (cc j) -> p cc j", cc=8)
        c.inc(gpsimd.dma_start(out=dst, in_=outwT.rearrange("(cc p) j -> p cc j", p=128)),
              p + "wol", 16)

    @c.block.tensor
    def _(tensor):
        c.wait(tensor, p + "nrm", nrm_total)
        for t in range(NT):
            if t >= 1:
                c.wait(tensor, p + "be", be0 + t)
            pT = (c.pTa, c.pTb)[t % 2]
            for ch in range(8):
                ins = nc.tensor.transpose(
                    pT[:, 128 * ch:128 * (ch + 1)],
                    aout[:, 1024 * t + 128 * ch:1024 * t + 128 * (ch + 1)],
                    c.ident[:])
            c.inc(ins, p + "bt2", 1)

    @c.block.vector
    def _(vector):
        c.wait(vector, p + "qev", qev_total)
        for t in range(NT):
            c.wait(vector, p + "bt2", bt0 + t + 1)
            dst = AP(aoutT, 128 * t, [[HCH * SEQ, 128], [SEQ, 8], [1, 128]])
            c.inc(nc.vector.tensor_copy(dst, (c.pTa, c.pTb)[t % 2][:].rearrange(
                "q (ch t) -> q ch t", ch=8)), p + "be", 1)

    @c.block.tensor
    def _(tensor):
        c.wait(tensor, p + "wol", 16)
        for t in range(NT):
            c.wait(tensor, p + "be", be0 + t + 1)
            if t >= 2:
                c.wait(tensor, p + "eps", eps0 + t - 1)
            for e in range(2):
                ps = ((c.pSa[:, 0:512], c.pSa[:, 512:1024]) if t % 2 == 0
                      else (c.pSb[:, 0:512], c.pSb[:, 512:1024]))[e]
                for cc in range(8):
                    ins = nc.tensor.matmul(
                        ps[:],
                        aoutT[:, cc * SEQ + 128 * t:cc * SEQ + 128 * (t + 1)],
                        wbuf[:, 1024 * cc + 512 * e:1024 * cc + 512 * (e + 1)],
                        start=(cc == 0), stop=(cc == 7))
            c.inc(ins, p + "wm", 1)

    @c.block.sync
    def _(sync):
        c.wait(sync, p + "nrm", nrm_total)
        for t in range(NT + 2):
            if t >= 2:
                c.wait(sync, p + "epv", t - 1)
            if t < NT:
                c.inc(sync.dma_start(out=ep_x[:, 1024 * (t % 2):1024 * (t % 2 + 1)],
                                     in_=x_src[128 * t:128 * (t + 1), :]),
                      p + "epl" + str(t % 2), 16)
            if t >= 2:
                ts = t - 2
                c.inc(sync.dma_start(out=x_dst[128 * ts:128 * (ts + 1), :],
                                     in_=ep_y[:, 1024 * (ts % 2):1024 * (ts % 2 + 1)]),
                      p + "epw" + str(ts % 2), 16)
        c.wait(sync, p + "epw0")
        c.wait(sync, p + "epw1")
        c.inc(sync.nop(), f"s{si}ep", 1)

    @c.block.vector
    def _(vector):
        c.wait(vector, p + "obr")
        for t in range(NT):
            c.wait(vector, p + "wm", wm0 + t + 1)
            psf = (c.pSa, c.pSb)[t % 2]
            nc.vector.drain()
            c.inc(nc.vector.tensor_tensor(ep_tmp[:], psf[:], outb_rep[:], ALU.add),
                  p + "eps", 1)
            c.wait(vector, p + "epl" + str(t % 2), epl0[t % 2] + 16 * (t // 2 + 1))
            if t >= 2:
                c.wait(vector, p + "epw" + str(t % 2), 16 * ((t - 2) // 2 + 1))
            nc.vector.drain()
            c.inc(nc.vector.tensor_tensor(ep_y[:, 1024 * (t % 2):1024 * (t % 2 + 1)],
                                          ep_tmp[:], ep_x[:, 1024 * (t % 2):1024 * (t % 2 + 1)],
                                          ALU.add), p + "epv", 1)
